# version 02
# baseline (speedup 1.0000x reference)
import numpy as np
import ml_dtypes
from contextlib import ExitStack

import concourse.bass as bass
import concourse.mybir as mybir
from concourse.bass_utils import run_bass_kernel_spmd

F32 = mybir.dt.float32
BF16 = mybir.dt.bfloat16
I32 = mybir.dt.int32
I8 = mybir.dt.int8
F16 = mybir.dt.float16
AF = mybir.ActivationFunctionType
ALU = mybir.AluOpType
AXL = mybir.AxisListType
BF = ml_dtypes.bfloat16

T, D, NCORE, NCH, CH = 16384, 2048, 8, 4, 512
DT = D // 128
COLS = DT * CH
H, HK, HD = 16, 4, 128
DKV = HK * HD
INTER0, LUTN = 5504, 16
IT0 = INTER0 // 128
INTER1 = 8192
IT1 = INTER1 // 128
EPS = 1e-5
SCALE0 = 1.0 / np.sqrt(HD)
SCALE1 = 1.0 / np.sqrt(2048.0)
SEG = 2048
SEGT = SEG // 128
SLOT_SEGS = [8, 6, 4, 2]
NSEG = sum(SLOT_SEGS)
NTILE_B = NSEG * SEGT
DEC = D // NCORE


def chunk_of_slot(core, s):
    return (31 - 8 * s) - core


def rank_slot_of_chunk(j):
    return (7 - j) % 8, 3 - j // 8


def host_masks_l0(core):
    p = np.arange(128)[:, None]
    f = np.arange(512)[None, :]
    m = np.zeros((NCH, 8, 128, 512), dtype=BF)
    for s in range(NCH):
        jg = chunk_of_slot(core, s)
        for j in range(8):
            if j < 4:
                m[s, j] = 0.0 if jg == 0 else (f < j * 128 + p).astype(BF)
            else:
                m[s, j] = ((j - 4) * 128 + p <= f).astype(BF)
    return m


def host_maskbank():
    p = np.arange(128)[:, None]
    f = np.arange(512)[None, :]
    b = np.zeros((6, 128, 512), dtype=BF)
    b[0] = 1.0
    b[1] = 0.0
    for d in range(4):
        b[2 + d] = (p + d * 128 <= f).astype(BF)
    return b


def host_cpos(core):
    p = np.arange(128)[:, None]
    cols = []
    for s in range(NCH):
        jq = chunk_of_slot(core, s)
        for r in range(SLOT_SEGS[s]):
            for u in range(SEGT):
                t = r * SEGT + u
                cols.append(np.clip(t * 128 + p - jq * CH, -1024, 1024))
    return np.clip(np.concatenate(cols, axis=1), -1024, 1024).astype(np.float16)


def host_frow():
    return np.broadcast_to(np.arange(512, dtype=np.float16)[None, :], (128, 512)).copy()


ENGINES = ("sync", "gpsimd", "tensor", "vector", "scalar")


class Sem:
    __slots__ = ("h", "val", "name")

    def __init__(self, h, name):
        self.h, self.val, self.name = h, 0, name


class Buf:
    __slots__ = ("name", "prod", "reads", "dram", "dsem")

    def __init__(self, name, dram=False):
        self.name, self.prod, self.reads = name, None, []
        self.dram, self.dsem = dram, {}


class Prog:
    def __init__(self):
        self.q = {e: [] for e in ENGINES}
        self.lastw = {e: {} for e in ENGINES}
        self.esem = {}
        self.dsem = {}
        self.mk_sem = None
        self.bufsems = []

    def wait(self, eng, sem, val):
        if sem is None or val is None or val <= 0:
            return
        lw = self.lastw[eng]
        if lw.get(sem.name, -1) >= val:
            return
        lw[sem.name] = val
        h = sem.h
        self.q[eng].append(lambda e, h=h, v=val: e.wait_ge(h, v))

    def do(self, eng, fn, reads=(), writes=(), inc_n=1, sem=None):
        for b in reads:
            if b.prod:
                self.wait(eng, *b.prod)
        for b in writes:
            if b.prod:
                self.wait(eng, *b.prod)
            for r in b.reads:
                self.wait(eng, *r)
        if sem is None and inc_n == 16:
            qt = "sw" if eng == "gpsimd" else "hw"
            cand = [b for b in list(writes) + list(reads) if not b.dram]
            if cand:
                b0 = cand[0]
                if qt not in b0.dsem:
                    b0.dsem[qt] = self.mk_sem(f"bds_{qt}_{b0.name}")
                    self.bufsems.append(b0.dsem[qt])
                sem = b0.dsem[qt]
            else:
                sem = self.dsem[eng]
        s = sem or (self.dsem[eng] if inc_n == 16 else self.esem[eng])
        tgt = s.val + inc_n
        s.val = tgt
        h = s.h

        def wrapped(e, fn=fn, h=h, n=inc_n):
            fn(e).then_inc(h, n)
        self.q[eng].append(wrapped)
        edge = (s, tgt)
        for b in writes:
            b.prod = edge
            b.reads = []
        for b in reads:
            b.reads.append(edge)
        return edge

    def barrier(self, extra=(), skip=()):
        sems = (list(self.esem.values()) + list(self.dsem.values())
                + self.bufsems + list(extra))
        for e in ENGINES:
            if e in skip:
                continue
            for s in sems:
                self.wait(e, s, s.val)


def emit_block(nc, prog):
    with nc.Block() as block:
        @block.sync
        def _(e):
            for f in prog.q["sync"]:
                f(e)

        @block.gpsimd
        def _(e):
            for f in prog.q["gpsimd"]:
                f(e)

        @block.tensor
        def _(e):
            for f in prog.q["tensor"]:
                f(e)

        @block.vector
        def _(e):
            for f in prog.q["vector"]:
                f(e)

        @block.scalar
        def _(e):
            for f in prog.q["scalar"]:
                f(e)


def dview(t, off, dims):
    return bass.AP(t, off, [list(d) for d in dims])


DTBYTES = {F32: 4, BF16: 2, I32: 4, I8: 1, F16: 2}


class Mem:
    def __init__(self, nc, base, top):
        self.nc, self.off, self.top = nc, base, top

    def take(self, name, shape, dtype):
        nb = (shape[-1] * DTBYTES[dtype] + 31) // 32 * 32
        assert self.off + nb <= self.top, f"SBUF overflow at {name}: {self.off}+{nb}"
        t = self.nc.alloc_sbuf_tensor_at(name, list(shape), dtype, offset=self.off)
        self.off += nb
        return t

    def mark(self):
        return self.off

    def reset(self, off):
        self.off = off


class Ctx:
    pass


def mm_group(ctx, psum_ap, terms, reads, wbuf, start=True, stop=True):
    n = len(terms)

    def fn(e):
        last = None
        for i, (lt, rt) in enumerate(terms):
            last = e.matmul(psum_ap, lt, rt, start=(start and i == 0),
                            stop=(stop and i == n - 1), skip_group_check=True)
        return last
    return ctx.p.do("tensor", fn, reads=reads, writes=(wbuf,))


def emit_ln(ctx, xT_sb, b_xT, out_xn, b_xn, gamma=None, beta=None, b_gb=None,
            out_u=None, b_u=None):
    p, c = ctx.p, ctx
    xb, b_xb = c.ln_xb, c.b_ln_xb
    st, b_st = c.ln_st, c.b_ln_st
    aB, b_aB = c.ln_aB, c.b_ln_aB
    bB, b_bB = c.ln_bB, c.b_ln_bB
    scr, b_scr = c.ln_scr, c.b_ln_scr
    ms, b_ms = c.ms_ps, c.b_ms

    p.do("vector", lambda e: e.tensor_copy(xb[:, :], xT_sb[:, :]), reads=(b_xT,), writes=(b_xb,))
    mm_group(ctx, ms[0:1, 0:CH],
             [(c.ones_col_bf[:, :], xb[:, dt * CH:(dt + 1) * CH]) for dt in range(DT)],
             (b_xb, c.b_ones), b_ms)
    p.do("vector", lambda e: e.tensor_copy(st[0:1, 0:CH], ms[0:1, 0:CH]), reads=(b_ms,), writes=(b_st,))
    p.do("vector", lambda e: e.tensor_tensor(xb[:, :], xb[:, :], xb[:, :], ALU.mult),
         reads=(b_xb,), writes=(b_xb,))
    mm_group(ctx, ms[0:1, 0:CH],
             [(c.ones_col_bf[:, :], xb[:, dt * CH:(dt + 1) * CH]) for dt in range(DT)],
             (b_xb, c.b_ones), b_ms)
    p.do("vector", lambda e: e.tensor_copy(st[0:1, CH:2 * CH], ms[0:1, 0:CH]), reads=(b_ms,), writes=(b_st,))
    p.do("vector", lambda e: e.tensor_scalar_mul(st[0:1, 0:CH], st[0:1, 0:CH], 1.0 / D), reads=(b_st,), writes=(b_st,))
    p.do("vector", lambda e: e.tensor_scalar_mul(st[0:1, CH:2 * CH], st[0:1, CH:2 * CH], 1.0 / D), reads=(b_st,), writes=(b_st,))
    p.do("vector", lambda e: e.tensor_tensor(st[0:1, 2 * CH:3 * CH], st[0:1, 0:CH], st[0:1, 0:CH], ALU.mult), reads=(b_st,), writes=(b_st,))
    p.do("vector", lambda e: e.tensor_tensor(st[0:1, CH:2 * CH], st[0:1, CH:2 * CH], st[0:1, 2 * CH:3 * CH], ALU.subtract), reads=(b_st,), writes=(b_st,))
    p.do("vector", lambda e: e.tensor_scalar_add(st[0:1, CH:2 * CH], st[0:1, CH:2 * CH], float(EPS)), reads=(b_st,), writes=(b_st,))
    p.do("scalar", lambda e: e.activation(st[0:1, 2 * CH:3 * CH], st[0:1, CH:2 * CH], AF.Ln, bias=0.0, scale=1.0), reads=(b_st,), writes=(b_st,))
    p.do("scalar", lambda e: e.activation(st[0:1, 2 * CH:3 * CH], st[0:1, 2 * CH:3 * CH], AF.Exp, bias=0.0, scale=-0.5), reads=(b_st,), writes=(b_st,))
    p.do("vector", lambda e: e.tensor_tensor(st[0:1, 3 * CH:4 * CH], st[0:1, 0:CH], st[0:1, 2 * CH:3 * CH], ALU.mult), reads=(b_st,), writes=(b_st,))
    p.do("vector", lambda e: e.tensor_scalar_mul(st[0:1, 3 * CH:4 * CH], st[0:1, 3 * CH:4 * CH], -1.0), reads=(b_st,), writes=(b_st,))
    mm_group(ctx, ms[:, 0:CH], [(c.ones_row_f32[0:1, :], st[0:1, 2 * CH:3 * CH])], (b_st, c.b_ones), b_ms)
    p.do("vector", lambda e: e.tensor_copy(aB[:, :], ms[:, 0:CH]), reads=(b_ms,), writes=(b_aB,))
    mm_group(ctx, ms[:, 0:CH], [(c.ones_row_f32[0:1, :], st[0:1, 3 * CH:4 * CH])], (b_st, c.b_ones), b_ms)
    p.do("vector", lambda e: e.tensor_copy(bB[:, :], ms[:, 0:CH]), reads=(b_ms,), writes=(b_bB,))
    for dt in range(DT):
        c0, c1 = dt * CH, (dt + 1) * CH
        if out_u is not None:
            p.do("vector", lambda e, c0=c0, c1=c1: e.tensor_tensor(out_u[:, c0:c1], xT_sb[:, c0:c1], aB[:, :], ALU.mult),
                 reads=(b_xT, b_aB), writes=(b_u,))
            p.do("vector", lambda e, c0=c0, c1=c1: e.tensor_tensor(out_u[:, c0:c1], out_u[:, c0:c1], bB[:, :], ALU.add),
                 reads=(b_bB,), writes=(b_u,))
        elif gamma is None:
            p.do("vector", lambda e, c0=c0, c1=c1: e.tensor_tensor(scr[:, 0:CH], xT_sb[:, c0:c1], aB[:, :], ALU.mult),
                 reads=(b_xT, b_aB), writes=(b_scr,))
            p.do("vector", lambda e, c0=c0, c1=c1: e.tensor_tensor(out_xn[:, c0:c1], scr[:, 0:CH], bB[:, :], ALU.add),
                 reads=(b_scr, b_bB), writes=(b_xn,))
        else:
            p.do("vector", lambda e, c0=c0, c1=c1: e.tensor_tensor(scr[:, 0:CH], xT_sb[:, c0:c1], aB[:, :], ALU.mult),
                 reads=(b_xT, b_aB), writes=(b_scr,))
            p.do("vector", lambda e, c0=c0, c1=c1: e.tensor_tensor(scr[:, 0:CH], scr[:, 0:CH], bB[:, :], ALU.add),
                 reads=(b_bB,), writes=(b_scr,))
            p.do("vector", lambda e, c0=c0, c1=c1, dt=dt: e.tensor_scalar(
                out_xn[:, c0:c1], scr[:, 0:CH], gamma[:, dt:dt + 1], beta[:, dt:dt + 1], ALU.mult, ALU.add),
                reads=(b_scr, b_gb), writes=(b_xn,))


def build(upto=4, mode="fused"):
    LA = mode == "LA"
    L3 = mode == "L3"
    FUSED = mode == "fused"
    L1 = L2 = False
    A_ANY = FUSED or LA
    nc = bass.Bass("TRN2", target_bir_lowering=False)
    es = ExitStack()
    ctx = Ctx()
    ctx.nc = nc
    p = Prog()
    ctx.p = p
    for e in ENGINES:
        p.esem[e] = Sem(es.enter_context(nc.semaphore("sem_" + e)), "sem_" + e)
        p.dsem[e] = Sem(es.enter_context(nc.semaphore("dma_" + e)), "dma_" + e)
    p.mk_sem = lambda nm: Sem(es.enter_context(nc.semaphore(nm)), nm)
    s_cc = Sem(es.enter_context(nc.semaphore("sem_cc")), "sem_cc")

    DI = lambda n, s, d: nc.dram_tensor(n, s, d, kind="ExternalInput")
    DN = lambda n, s, d: nc.dram_tensor(n, s, d, kind="Internal")
    DO = lambda n, s, d: nc.dram_tensor(n, s, d, kind="ExternalOutput")
    x_in = DI("x_in", [NCH, 2, CH, D], F32)
    wq = DI("wq", [D, D], F32)
    wk = DI("wk", [D, DKV], F32)
    wv = DI("wv", [D, DKV], F32)
    wo = DI("wo", [D, D], F32)
    masks0 = DI("masks0", [NCH, 8, 128, 512], BF16)
    ident = DI("ident", [128, 128], F32)
    lut_t = DI("lut", [1, LUTN], F32)
    gidx = DI("gidx", [D, INTER0], I32)
    uidx = DI("uidx", [D, INTER0], I32)
    didx = DI("didx", [INTER0, D], I32)
    qkv_t = DI("qkv", [D, 3 * D], F32)
    oproj_t = DI("oproj", [D, D], F32)
    gam_t = DI("gam", [128, DT], F32)
    bet_t = DI("bet", [128, DT], F32)
    wg1 = DI("wg1", [D, INTER1], I8)
    wu1 = DI("wu1", [D, INTER1], I8)
    wd1 = DI("wd1", [INTER1, D], I8)
    sg1_t = DI("sg1", [128, IT1], F32)
    su1_t = DI("su1", [128, IT1], F32)
    sd1_t = DI("sd1", [128, DT], F32)
    cpos_t = DI("cpos", [128, NTILE_B], F16)
    frow_t = DI("frow", [128, 512], F16)

    gwt_loc = DN("gwt_loc", [D, INTER0], BF16) if A_ANY else None
    uwt_loc = DN("uwt_loc", [D, INTER0], BF16) if A_ANY else None
    dwt_loc = DN("dwt_loc", [INTER0, D], BF16) if A_ANY else None
    kt_loc = (DO if LA else DN)("kt_loc", [D, NCH * CH], BF16) if A_ANY else None
    v_loc = (DO if LA else DN)("v_loc", [NCH * CH, D], BF16) if A_ANY else None
    stats_loc = (DO if LA else DN)("stats_loc", [1, 128], F32) if A_ANY else None
    kt_g = (DI if L3 else DN)("kt_g", [NCORE * D, NCH * CH], BF16) if (FUSED or L3) else None
    v_g = (DI if L3 else DN)("v_g", [NCORE * NCH * CH, D], BF16) if (FUSED or L3) else None
    stats_g = (DI if L3 else DN)("stats_g", [NCORE, 128], F32) if (FUSED or L3) else None
    u_hbm = (DO if LA else (DI if L3 else DN))("u_hbm", [NCH, 128, COLS], F32)
    qt_hbm = (DO if LA else (DI if L3 else DN))("qt_hbm", [NCH, 128, COLS], BF16)
    x1_hbm = (DO if upto == 1 else DN)("x1_hbm", [NCH, 128, COLS], F32) if (A_ANY or upto == 1) else None
    x2_hbm = (DO if (upto == 2 or LA) else (DI if L3 else DN))("x2_hbm", [NCH, 128, COLS], F32)
    x3_hbm = ((DO if upto == 3 else DN)("x3_hbm", [NCH, 128, COLS], F32)
              if (upto >= 3 and not LA) else None)
    out_rows = DO("out_rows", [NCH, CH, D], F32) if (upto >= 4 and not LA) else None

    b_gwtl, b_uwtl, b_dwtl = Buf("gwtl", 1), Buf("uwtl", 1), Buf("dwtl", 1)
    b_wtg = Buf("wtg", 1)
    b_ktl, b_vl, b_statl = Buf("ktl", 1), Buf("vl", 1), Buf("statl", 1)
    b_kvg = Buf("kvg", 1)
    b_uh, b_qth, b_x2h, b_x3h = Buf("uh", 1), Buf("qth", 1), Buf("x2h", 1), Buf("x3h", 1)
    b_x1h = Buf("x1h", 1)

    mem = Mem(nc, 16512, 229344)
    tk = mem.take
    ctx.ones_col_bf = tk("ones_col_bf", [128, 1], BF16)
    ctx.ones_row_f32 = tk("ones_row_f32", [1, 128], F32)
    ctx.id_sb = tk("id_sb", [128, 128], F32)
    lut_b = tk("lut_b", [128, LUTN], F32)
    gam_sb = tk("gam_sb", [128, DT], F32)
    bet_sb = tk("bet_sb", [128, DT], F32)
    sg1_sb = tk("sg1_sb", [128, IT1], F32)
    su1_sb = tk("su1_sb", [128, IT1], F32)
    sd1_sb = tk("sd1_sb", [128, DT], F32)
    lutrow = tk("lutrow", [1, LUTN], F32)
    scrow = tk("scrow", [1, 64], F32)
    linv = tk("linv", [1, CH], F32)
    linvB = tk("linvB", [128, CH], F32)
    negm = tk("negm", [128, 1], F32)
    stats_sb = tk("stats_sb", [1, 128], F32)
    stgb = [tk(f"stgb{i}", [128, CH], BF16) for i in range(2)]
    wcol = [tk(f"wcol{i}", [128, DT * 128], BF16) for i in range(4)]
    ctx.ln_xb = tk("ln_xb", [128, COLS], BF16)
    ctx.ln_st = tk("ln_st", [1, 4 * CH], F32)
    ctx.ln_aB = tk("ln_aB", [128, CH], F32)
    ctx.ln_bB = tk("ln_bB", [128, CH], F32)
    ctx.ln_scr = tk("ln_scr", [128, CH], F32)
    for nm in ("ln_xb", "ln_st", "ln_aB", "ln_bB", "ln_scr"):
        setattr(ctx, "b_" + nm, Buf(nm))

    arena = mem.mark()
    xT = tk("xT", [128, COLS], F32)
    xn = tk("xn", [128, COLS], BF16)
    xin_sb = tk("xin_sb", [128, D], F32)
    solid_end = mem.mark()
    wv_sb = tk("wv_sb", [128, DT * DKV], BF16)
    kt_prev = tk("kt_prev", [128, HK * CH], BF16)
    kt_cur = tk("kt_cur", [128, HK * CH], BF16)
    v_prev = tk("v_prev", [128, 4 * DKV], BF16)
    v_cur = tk("v_cur", [128, 4 * DKV], BF16)
    qt0 = tk("qt0", [128, H * CH], BF16)
    ptb = tk("ptb", [128, 8 * CH], BF16)
    mask8 = tk("mask8", [128, 8 * CH], BF16)
    attn_out = tk("attn_out", [128, H * CH], BF16)
    DW = 1024
    DWP = 512
    dec_v = [tk(f"dec_v{i}", [128, DW], BF16) for i in range(4)]
    dec_p = [tk(f"dec_p{i}", [128, DWP], BF16) for i in range(4)]
    g1_end = mem.mark()
    mem.reset(solid_end)
    h0 = tk("h0", [128, IT0 * CH], BF16)
    wdcolA = [tk(f"wdcolA{i}", [128, IT0 * 128], BF16) for i in range(2)]
    g2_end = mem.mark()
    gd_end = mem.mark()
    mem.reset(solid_end)
    u_sb = tk("u_sb", [128, COLS], F32)
    vslab = tk("vslab", [128, DT * CH], BF16)
    g3_end = mem.mark()
    mem.reset(arena)
    qtb = tk("qtb", [128, COLS], BF16)
    pts = tk("pts", [128, SEGT * CH], BF16)
    acc = tk("acc", [128, DT * CH], F32)
    vbig = tk("vbig", [128, SEGT * DT * 128], BF16)
    ktile = [tk(f"ktile{i}", [128, DT * 128], BF16) for i in range(2)]
    cpos_sb = tk("cpos_sb", [128, NTILE_B], F16)
    frow_sb = tk("frow_sb", [128, 512], F16)
    xu_t = [tk(f"xu_t{i}", [128, CH], F32) for i in range(4)]
    x3stg = [tk(f"x3stg{i}", [128, CH], F32) for i in range(2)]
    b_end = mem.mark()
    mem.reset(solid_end)
    h1 = tk("h1", [128, IT1 * CH], BF16)
    wdcolC = [tk(f"wdcolC{i}", [128, IT1 * 128], BF16) for i in range(2)]
    xout = xin_sb
    c_end = mem.mark()
    assert max(g1_end, g2_end, gd_end, g3_end, b_end, c_end) <= 229344, \
        (g1_end, g2_end, gd_end, g3_end, b_end, c_end)

    pm = lambda n: es.enter_context(nc.psum_tensor(n, [128, CH], F32))
    mm_ps = [pm("mm_ps0"), pm("mm_ps1")]
    av_ps = [pm("av_ps0"), pm("av_ps1")]
    ctx.ms_ps = pm("ms_ps")
    l_ps = pm("l_ps")
    tpb_ps = es.enter_context(nc.psum_tensor("tpb_ps", [128, 128], BF16))

    ctx.b_ones, ctx.b_id, ctx.b_ms = Buf("ones"), Buf("id"), Buf("ms")
    b_idbf = Buf("idbf")
    b_lut, b_gb, b_sc1 = Buf("lut"), Buf("gb"), Buf("sc1")
    b_scrow, b_linv, b_linvB, b_negm = Buf("scrow"), Buf("linv"), Buf("linvB"), Buf("negm")
    b_cpos, b_frow, b_stats = Buf("cpos"), Buf("frow"), Buf("stats")
    b_stgb = [Buf("stgb0"), Buf("stgb1")]
    b_wcol = [Buf(f"wcol{i}") for i in range(4)]
    b_xT, b_xn, b_xin = Buf("xT"), Buf("xn"), Buf("xin")
    b_wv, b_ktp, b_ktc, b_vp, b_vc = Buf("wv"), Buf("ktp"), Buf("ktc"), Buf("vp"), Buf("vc")
    b_qt0, b_ptb, b_m8, b_ao = Buf("qt0"), Buf("ptb"), Buf("m8"), Buf("ao")
    b_h0 = Buf("h0")
    b_wdA = [Buf("wdA0"), Buf("wdA1")]
    b_dec_v = [Buf(f"dec_v{i}") for i in range(4)]
    b_dec_p = [Buf(f"dec_p{i}") for i in range(4)]
    b_u, b_vslab = Buf("u"), Buf("vslab")
    b_qtb, b_pts, b_acc, b_vbig = Buf("qtb"), Buf("pts"), Buf("acc"), Buf("vbig")
    b_ktile = [Buf("ktile0"), Buf("ktile1")]
    b_xu = [Buf(f"xu{i}") for i in range(4)]
    b_x3s = [Buf("x3s0"), Buf("x3s1")]
    b_h1 = Buf("h1")
    b_xout = None
    b_wdC = [Buf("wdC0"), Buf("wdC1")]
    b_xout = b_xin
    b_mm = [Buf("mm0"), Buf("mm1")]
    b_av = [Buf("av0"), Buf("av1")]
    b_lp, b_tpb = Buf("lp"), Buf("tpb")

    p.do("gpsimd", lambda e: e.memset(ctx.ones_col_bf[:, :], 1.0), writes=(ctx.b_ones,))
    p.do("gpsimd", lambda e: e.memset(ctx.ones_row_f32[:, :], 1.0), writes=(ctx.b_ones,))
    p.do("sync", lambda e: e.dma_start(out=ctx.id_sb[:, :], in_=ident.ap()), writes=(ctx.b_id,), inc_n=16)
    p.do("sync", lambda e: e.dma_start(out=gam_sb[:, :], in_=gam_t.ap()), writes=(b_gb,), inc_n=16)
    p.do("sync", lambda e: e.dma_start(out=bet_sb[:, :], in_=bet_t.ap()), writes=(b_gb,), inc_n=16)
    p.do("sync", lambda e: e.dma_start(out=sg1_sb[:, :], in_=sg1_t.ap()), writes=(b_sc1,), inc_n=16)
    p.do("sync", lambda e: e.dma_start(out=su1_sb[:, :], in_=su1_t.ap()), writes=(b_sc1,), inc_n=16)
    p.do("sync", lambda e: e.dma_start(out=sd1_sb[:, :], in_=sd1_t.ap()), writes=(b_sc1,), inc_n=16)
    p.do("sync", lambda e: e.dma_start(out=lutrow[0:1, :], in_=lut_t.ap()), writes=(b_lut,), inc_n=16)
    mm_group(ctx, ctx.ms_ps[:, 0:LUTN], [(ctx.ones_row_f32[0:1, :], lutrow[0:1, :])],
             (b_lut, ctx.b_ones), ctx.b_ms)
    p.do("vector", lambda e: e.tensor_copy(lut_b[:, :], ctx.ms_ps[:, 0:LUTN]), reads=(ctx.b_ms,), writes=(b_lut,))


    decode_units = []

    def mk_unit(idx_t, out_t, b_out, ht, c0, w, eng, k):
        bufs = b_dec_v if eng == "vector" else b_dec_p
        tiles = dec_v if eng == "vector" else dec_p
        ib = k % 2
        ncols = INTER0 if out_t is gwt_loc or out_t is uwt_loc else D

        def emit():
            p.do("gpsimd", lambda e: e.dma_start(
                out=tiles[ib][:, 0:w],
                in_=dview(idx_t, ht * 128 * ncols + c0, [[ncols, 128], [1, w]])),
                writes=(bufs[ib],), inc_n=16)
            for t in range(LUTN):
                if t == 0:
                    p.do(eng, lambda e: e.tensor_scalar(
                        tiles[2][:, 0:w], tiles[ib][:, 0:w], 0.0, lut_b[:, 0:1],
                        ALU.is_equal, ALU.mult),
                        reads=(bufs[ib], b_lut), writes=(bufs[2],))
                else:
                    p.do(eng, lambda e, t=t: e.tensor_scalar(
                        tiles[3][:, 0:w], tiles[ib][:, 0:w], float(t), lut_b[:, t:t + 1],
                        ALU.is_equal, ALU.mult),
                        reads=(bufs[ib], b_lut), writes=(bufs[3],))
                    p.do(eng, lambda e: e.tensor_tensor(
                        tiles[2][:, 0:w], tiles[2][:, 0:w], tiles[3][:, 0:w], ALU.add),
                        reads=(bufs[3],), writes=(bufs[2],))
            p.do("scalar", lambda e: e.dma_start(
                out=dview(out_t, ht * 128 * ncols + c0, [[ncols, 128], [1, w]]),
                in_=tiles[2][:, 0:w]),
                reads=(bufs[2],), writes=(b_out,), inc_n=16)
        return emit

    if A_ANY:
        k = 0
        for (idx_t, out_t, b_out, nht, ncols) in (
                (gidx, gwt_loc, b_gwtl, DT, INTER0),
                (uidx, uwt_loc, b_uwtl, DT, INTER0),
                (didx, dwt_loc, b_dwtl, IT0, D)):
            for ht in range(nht):
                c0 = 0
                while c0 < ncols:
                    eng = "gpsimd" if k % 3 == 2 else "vector"
                    w = min(DWP if eng == "gpsimd" else DW, ncols - c0)
                    decode_units.append(mk_unit(idx_t, out_t, b_out, ht, c0, w, eng, k))
                    c0 += w
                    k += 1

    def pump(n):
        while n > 0 and decode_units:
            decode_units.pop(0)()
            n -= 1

    agw_pending = [False]
    if A_ANY:
        p.do("gpsimd", lambda e: e.dma_start(
            out=dview(wv_sb, 0, [[DT * DKV, 128], [DKV, DT], [1, DKV]]),
            in_=dview(wv, 0, [[DKV, 128], [128 * DKV, DT], [1, DKV]])),
            writes=(b_wv,), inc_n=16)

    def transpose_in(ntt, tt_base, dst, b_dst):
        for tt in range(ntt):
            for dt in range(DT):
                b = (tt * DT + dt) % 2
                p.do("tensor", lambda e, tt=tt, dt=dt, b=b: e.transpose(
                    mm_ps[b][:, 0:128],
                    xin_sb[:, tt * D + dt * 128: tt * D + (dt + 1) * 128],
                    ctx.id_sb[:, :]),
                    reads=(b_xin, ctx.b_id), writes=(b_mm[b],))
                to = tt_base + tt
                p.do("vector", lambda e, dt=dt, to=to, b=b: e.tensor_copy(
                    dst[:, dt * CH + to * 128: dt * CH + (to + 1) * 128], mm_ps[b][:, 0:128]),
                    reads=(b_mm[b],), writes=(b_dst,))

    def load_and_transpose(s, part, dst, b_dst):
        for q in range(4):
            p.do("sync", lambda e, s=s, part=part, q=q: e.dma_start(
                out=xin_sb[:, :],
                in_=dview(x_in, ((s * 2 + part) * CH + q * 128) * D,
                          [[D, 128], [1, D]])),
                writes=(b_xin,), inc_n=16)
            transpose_in(1, q, dst, b_dst)

    def kv_proj(dst_kt, b_dkt, dst_v, b_dv):
        for g in range(HK):
            buf = g % 2
            p.do("gpsimd", lambda e, g=g, buf=buf: e.dma_start(
                out=dview(wcol[buf], 0, [[DT * 128, 128], [128, DT], [1, 128]]),
                in_=dview(wk, g * 128, [[DKV, 128], [128 * DKV, DT], [1, 128]])),
                writes=(b_wcol[buf],), inc_n=16)
            bk = g % 2
            mm_group(ctx, mm_ps[bk][:, :],
                     [(wcol[buf][:, dt * 128:(dt + 1) * 128], xn[:, dt * CH:(dt + 1) * CH])
                      for dt in range(DT)],
                     (b_wcol[buf], b_xn), b_mm[bk])
            p.do("vector", lambda e, g=g, bk=bk, dst_kt=dst_kt: e.tensor_copy(
                dst_kt[:, g * CH:(g + 1) * CH], mm_ps[bk][:, :]),
                reads=(b_mm[bk],), writes=(b_dkt,))
        for tt in range(4):
            bk = tt % 2
            mm_group(ctx, av_ps[bk][:, 0:DKV],
                     [(xn[:, dt * CH + tt * 128: dt * CH + tt * 128 + 128],
                       wv_sb[:, dt * DKV:(dt + 1) * DKV]) for dt in range(DT)],
                     (b_wv, b_xn), b_av[bk])
            p.do("vector", lambda e, tt=tt, bk=bk, dst_v=dst_v: e.tensor_copy(
                dst_v[:, tt * DKV:(tt + 1) * DKV], av_ps[bk][:, 0:DKV]),
                reads=(b_av[bk],), writes=(b_dv,))

    def block_maxsq(src, b_src, nblk, out_col):
        for b in range(nblk):
            mm_group(ctx, l_ps[0:1, 0:CH],
                     [(ctx.ones_col_bf[:, :], src[:, b * CH:(b + 1) * CH])],
                     (b_src, ctx.b_ones), b_lp)
            p.do("vector", lambda e, b=b: e.reduce_max(scrow[0:1, 32 + b:33 + b], l_ps[0:1, 0:CH], AXL.X),
                 reads=(b_lp,), writes=(b_scrow,))
        p.do("vector", lambda e, nblk=nblk, out_col=out_col: e.reduce_max(
            scrow[0:1, out_col:out_col + 1], scrow[0:1, 32:32 + nblk], AXL.X),
            reads=(b_scrow,), writes=(b_scrow,))

    def emit_negm_from(cols):
        p.do("vector", lambda e, cols=cols: e.tensor_tensor(
            scrow[0:1, 2:3], scrow[0:1, cols[0]:cols[0] + 1], scrow[0:1, cols[1]:cols[1] + 1], ALU.mult),
            reads=(b_scrow,), writes=(b_scrow,))
        p.do("scalar", lambda e: e.activation(scrow[0:1, 3:4], scrow[0:1, 2:3], AF.Sqrt, bias=0.0, scale=1.0),
             reads=(b_scrow,), writes=(b_scrow,))
        p.do("vector", lambda e: e.tensor_scalar_mul(scrow[0:1, 4:5], scrow[0:1, 3:4], -1.0),
             reads=(b_scrow,), writes=(b_scrow,))
        mm_group(ctx, l_ps[:, 0:1], [(ctx.ones_row_f32[0:1, :], scrow[0:1, 4:5])],
                 (b_scrow, ctx.b_ones), b_lp)
        p.do("vector", lambda e: e.tensor_copy(negm[:, :], l_ps[:, 0:1]), reads=(b_lp,), writes=(b_negm,))

    for s in range(NCH):
      if L3:
        continue
      if A_ANY:
        for part in (0, 1):
            load_and_transpose(s, part, xT, b_xT)
            emit_ln(ctx, xT, b_xT, xn, b_xn)
            pump(8)
            if part == 0:
                kv_proj(kt_prev, b_ktp, v_prev, b_vp)
            else:
                kv_proj(kt_cur, b_ktc, v_cur, b_vc)
            pump(8)

        for g in range(H):
            buf = g % 2
            p.do("gpsimd", lambda e, g=g, buf=buf: e.dma_start(
                out=dview(wcol[buf], 0, [[DT * 128, 128], [128, DT], [1, 128]]),
                in_=dview(wq, g * 128, [[D, 128], [128 * D, DT], [1, 128]])),
                writes=(b_wcol[buf],), inc_n=16)
            bk = g % 2
            mm_group(ctx, mm_ps[bk][:, :],
                     [(wcol[buf][:, dt * 128:(dt + 1) * 128], xn[:, dt * CH:(dt + 1) * CH])
                      for dt in range(DT)],
                     (b_wcol[buf], b_xn), b_mm[bk])
            p.do("scalar", lambda e, g=g, bk=bk: e.activation(
                qt0[:, g * CH:(g + 1) * CH], mm_ps[bk][:, :], AF.Copy, bias=0.0, scale=float(SCALE0)),
                reads=(b_mm[bk],), writes=(b_qt0,))

        p.do("sync", lambda e, s=s: e.dma_start(
            out=mask8[:, :],
            in_=dview(masks0, s * 8 * 128 * 512, [[512, 128], [128 * 512, 8], [1, 512]])),
            writes=(b_m8,), inc_n=16)

        sqs, b_sqs = ctx.ln_xb, ctx.b_ln_xb
        p.do("vector", lambda e: e.tensor_tensor(sqs[:, 0:HK * CH], kt_prev[:, :], kt_prev[:, :], ALU.mult),
             reads=(b_ktp,), writes=(b_sqs,))
        p.do("vector", lambda e: e.tensor_tensor(sqs[:, HK * CH:2 * HK * CH], kt_cur[:, :], kt_cur[:, :], ALU.mult),
             reads=(b_ktc,), writes=(b_sqs,))
        block_maxsq(sqs, b_sqs, 2 * HK, 0)
        p.do("vector", lambda e: e.tensor_tensor(sqs[:, 0:H * CH], qt0[:, :], qt0[:, :], ALU.mult),
             reads=(b_qt0,), writes=(b_sqs,))
        block_maxsq(sqs, b_sqs, H, 1)
        emit_negm_from((0, 1))

        for g in range(H):
            kg = g // 4
            for j in range(8):
                bk = j % 2
                src, bsrc = (kt_prev, b_ktp) if j < 4 else (kt_cur, b_ktc)
                jj = j % 4
                mm_group(ctx, mm_ps[bk][:, :],
                         [(src[:, kg * CH + jj * 128: kg * CH + (jj + 1) * 128],
                           qt0[:, g * CH:(g + 1) * CH])],
                         (bsrc, b_qt0), b_mm[bk])
                p.do("scalar", lambda e, j=j, bk=bk: e.activation(
                    ptb[:, j * CH:(j + 1) * CH], mm_ps[bk][:, :], AF.Exp,
                    bias=negm[:, 0:1], scale=1.0),
                    reads=(b_mm[bk], b_negm), writes=(b_ptb,))
                p.do("vector", lambda e, j=j: e.tensor_tensor(
                    ptb[:, j * CH:(j + 1) * CH], ptb[:, j * CH:(j + 1) * CH],
                    mask8[:, j * CH:(j + 1) * CH], ALU.mult),
                    reads=(b_m8,), writes=(b_ptb,))
            mm_group(ctx, l_ps[0:1, 0:CH],
                     [(ctx.ones_col_bf[:, :], ptb[:, j * CH:(j + 1) * CH]) for j in range(8)],
                     (b_ptb, ctx.b_ones), b_lp)
            p.do("vector", lambda e: e.reciprocal(linv[0:1, :], l_ps[0:1, 0:CH]),
                 reads=(b_lp,), writes=(b_linv,))
            abk = g % 2
            terms = []
            for j in range(8):
                vsrc = v_prev if j < 4 else v_cur
                jj = j % 4
                terms.append((vsrc[:, jj * DKV + kg * 128: jj * DKV + (kg + 1) * 128],
                              ptb[:, j * CH:(j + 1) * CH]))
            mm_group(ctx, av_ps[abk][:, :], terms, (b_vp, b_vc, b_ptb), b_av[abk])
            mm_group(ctx, l_ps[:, 0:CH], [(ctx.ones_row_f32[0:1, :], linv[0:1, :])],
                     (b_linv, ctx.b_ones), b_lp)
            p.do("vector", lambda e: e.tensor_copy(linvB[:, :], l_ps[:, 0:CH]),
                 reads=(b_lp,), writes=(b_linvB,))
            p.do("vector", lambda e, g=g, abk=abk: e.tensor_tensor(
                attn_out[:, g * CH:(g + 1) * CH], av_ps[abk][:, :], linvB[:, :], ALU.mult),
                reads=(b_av[abk], b_linvB), writes=(b_ao,))

        for o in range(DT):
            buf = o % 2
            p.do("gpsimd", lambda e, o=o, buf=buf: e.dma_start(
                out=dview(wcol[buf], 0, [[DT * 128, 128], [128, DT], [1, 128]]),
                in_=dview(wo, o * 128, [[D, 128], [128 * D, DT], [1, 128]])),
                writes=(b_wcol[buf],), inc_n=16)
            bk = o % 2
            mm_group(ctx, mm_ps[bk][:, :],
                     [(wcol[buf][:, g * 128:(g + 1) * 128], attn_out[:, g * CH:(g + 1) * CH])
                      for g in range(DT)],
                     (b_wcol[buf], b_ao), b_mm[bk])
            p.do("vector", lambda e, o=o, bk=bk: e.tensor_tensor(
                xT[:, o * CH:(o + 1) * CH], xT[:, o * CH:(o + 1) * CH], mm_ps[bk][:, :], ALU.add),
                reads=(b_mm[bk],), writes=(b_xT,))
        p.do("scalar", lambda e, s=s: e.dma_start(
            out=dview(x1_hbm, s * 128 * COLS, [[COLS, 128], [1, COLS]]),
            in_=xT[:, :]), reads=(b_xT,), writes=(b_x1h,), inc_n=16)
        pump(16)
        p.barrier(skip=("vector", "gpsimd", "scalar"))

    if upto == 1:
        finish(p, es, nc)
        return nc
    pump(10 ** 9)

    for s in range(NCH):
      if L3:
        continue
      if A_ANY:
        p.do("sync", lambda e, s=s: e.dma_start(
            out=xT[:, :], in_=dview(x1_hbm, s * 128 * COLS, [[COLS, 128], [1, COLS]])),
            reads=(b_x1h,), writes=(b_xT,), inc_n=16)
        emit_ln(ctx, xT, b_xT, xn, b_xn)
        for it in range(IT0):
            gb, ub = it % 2, 2 + it % 2
            p.do("sync", lambda e, it=it, gb=gb: e.dma_start(
                out=dview(wcol[gb], 0, [[DT * 128, 128], [128, DT], [1, 128]]),
                in_=dview(gwt_loc, it * 128, [[INTER0, 128], [128 * INTER0, DT], [1, 128]])),
                reads=(b_gwtl,), writes=(b_wcol[gb],), inc_n=16)
            mm_group(ctx, mm_ps[0][:, :],
                     [(wcol[gb][:, dt * 128:(dt + 1) * 128], xn[:, dt * CH:(dt + 1) * CH])
                      for dt in range(DT)],
                     (b_wcol[gb], b_xn), b_mm[0])
            p.do("scalar", lambda e, it=it: e.activation(
                h0[:, it * CH:(it + 1) * CH], mm_ps[0][:, :], AF.Silu, bias=0.0, scale=1.0),
                reads=(b_mm[0],), writes=(b_h0,))
            p.do("sync", lambda e, it=it, ub=ub: e.dma_start(
                out=dview(wcol[ub], 0, [[DT * 128, 128], [128, DT], [1, 128]]),
                in_=dview(uwt_loc, it * 128, [[INTER0, 128], [128 * INTER0, DT], [1, 128]])),
                reads=(b_uwtl,), writes=(b_wcol[ub],), inc_n=16)
            mm_group(ctx, mm_ps[1][:, :],
                     [(wcol[ub][:, dt * 128:(dt + 1) * 128], xn[:, dt * CH:(dt + 1) * CH])
                      for dt in range(DT)],
                     (b_wcol[ub], b_xn), b_mm[1])
            p.do("vector", lambda e, it=it: e.tensor_tensor(
                h0[:, it * CH:(it + 1) * CH], h0[:, it * CH:(it + 1) * CH], mm_ps[1][:, :], ALU.mult),
                reads=(b_mm[1],), writes=(b_h0,))
        pump(10)
        for o in range(DT):
            db = o % 2
            p.do("sync", lambda e, o=o, db=db: e.dma_start(
                out=dview(wdcolA[db], 0, [[IT0 * 128, 128], [128, IT0], [1, 128]]),
                in_=dview(dwt_loc, o * 128, [[D, 128], [128 * D, IT0], [1, 128]])),
                reads=(b_dwtl,), writes=(b_wdA[db],), inc_n=16)
            bk = o % 2
            mm_group(ctx, mm_ps[bk][:, :],
                     [(wdcolA[db][:, ht * 128:(ht + 1) * 128], h0[:, ht * CH:(ht + 1) * CH])
                      for ht in range(IT0)],
                     (b_wdA[db], b_h0), b_mm[bk])
            p.do("vector", lambda e, o=o, bk=bk: e.tensor_tensor(
                xT[:, o * CH:(o + 1) * CH], xT[:, o * CH:(o + 1) * CH], mm_ps[bk][:, :], ALU.add),
                reads=(b_mm[bk],), writes=(b_xT,))
        p.do("scalar", lambda e, s=s: e.dma_start(
            out=dview(x2_hbm, s * 128 * COLS, [[COLS, 128], [1, COLS]]),
            in_=xT[:, :]), reads=(b_xT,), writes=(b_x2h,), inc_n=16)
        if upto == 2:
            p.barrier()
            continue
        p.barrier()

        emit_ln(ctx, xT, b_xT, None, None, out_u=u_sb, b_u=b_u)
        p.do("scalar", lambda e, s=s: e.dma_start(
            out=dview(u_hbm, s * 128 * COLS, [[COLS, 128], [1, COLS]]),
            in_=u_sb[:, :]), reads=(b_u,), writes=(b_uh,), inc_n=16)
        emit_ln(ctx, u_sb, b_u, xn, b_xn, gamma=gam_sb, beta=bet_sb, b_gb=b_gb)
        for proj in (0, 1):
            for o in range(DT):
                buf = o % 2
                p.do("gpsimd", lambda e, proj=proj, o=o, buf=buf: e.dma_start(
                    out=dview(wcol[buf], 0, [[DT * 128, 128], [128, DT], [1, 128]]),
                    in_=dview(qkv_t, proj * D + o * 128, [[3 * D, 128], [128 * 3 * D, DT], [1, 128]])),
                    writes=(b_wcol[buf],), inc_n=16)
                bk = o % 2
                mm_group(ctx, mm_ps[bk][:, :],
                         [(wcol[buf][:, dt * 128:(dt + 1) * 128], xn[:, dt * CH:(dt + 1) * CH])
                          for dt in range(DT)],
                         (b_wcol[buf], b_xn), b_mm[bk])
                sb_ = stgb[o % 2]
                bs = b_stgb[o % 2]
                p.do("scalar", lambda e, bk=bk, sb_=sb_, proj=proj: e.activation(
                    sb_[:, :], mm_ps[bk][:, :], AF.Copy, bias=0.0,
                    scale=float(SCALE1) if proj == 0 else 1.0),
                    reads=(b_mm[bk],), writes=(bs,))
                if proj == 0:
                    p.do("scalar", lambda e, s=s, o=o, sb_=sb_: e.dma_start(
                        out=dview(qt_hbm, s * 128 * COLS + o * CH, [[COLS, 128], [1, CH]]),
                        in_=sb_[:, :]), reads=(bs,), writes=(b_qth,), inc_n=16)
                else:
                    p.do("scalar", lambda e, s=s, o=o, sb_=sb_: e.dma_start(
                        out=dview(kt_loc, (s * 4 * 128) * D + o * 128,
                                  [[D, 128], [128 * D, 4], [1, 128]]),
                        in_=sb_[:, :]), reads=(bs,), writes=(b_ktl,), inc_n=16)
                    p.do("vector", lambda e, sb_=sb_: e.tensor_tensor(
                        ctx.ln_xb[:, 0:CH], sb_[:, :], sb_[:, :], ALU.mult),
                        reads=(bs,), writes=(ctx.b_ln_xb,))
                    mm_group(ctx, l_ps[0:1, 0:CH],
                             [(ctx.ones_col_bf[:, :], ctx.ln_xb[:, 0:CH])],
                             (ctx.b_ln_xb, ctx.b_ones), b_lp,
                             start=(o == 0), stop=(o == DT - 1))
        p.do("vector", lambda e: e.reduce_max(scrow[0:1, 8:9], l_ps[0:1, 0:CH], AXL.X),
             reads=(b_lp,), writes=(b_scrow,))
        if s == 0:
            p.do("vector", lambda e: e.tensor_copy(scrow[0:1, 9:10], scrow[0:1, 8:9]),
                 reads=(b_scrow,), writes=(b_scrow,))
        else:
            p.do("vector", lambda e: e.tensor_tensor(scrow[0:1, 9:10], scrow[0:1, 9:10], scrow[0:1, 8:9], ALU.max),
                 reads=(b_scrow,), writes=(b_scrow,))
        for dvc in range(4):
            p.do("gpsimd", lambda e, dvc=dvc: e.dma_start(
                out=dview(vslab, 0, [[DT * CH, 128], [CH, DT], [1, CH]]),
                in_=dview(qkv_t, 2 * D + dvc * CH, [[3 * D, 128], [128 * 3 * D, DT], [1, CH]])),
                writes=(b_vslab,), inc_n=16)
            for tt in range(4):
                bk = tt % 2
                mm_group(ctx, av_ps[bk][:, :],
                         [(xn[:, dt * CH + tt * 128: dt * CH + tt * 128 + 128],
                           vslab[:, dt * CH:(dt + 1) * CH]) for dt in range(DT)],
                         (b_vslab, b_xn), b_av[bk])
                sb_ = stgb[tt % 2]
                bs = b_stgb[tt % 2]
                p.do("scalar", lambda e, bk=bk, sb_=sb_: e.activation(
                    sb_[:, :], av_ps[bk][:, :], AF.Copy, bias=0.0, scale=1.0),
                    reads=(b_av[bk],), writes=(bs,))
                p.do("scalar", lambda e, s=s, tt=tt, dvc=dvc, sb_=sb_: e.dma_start(
                    out=dview(v_loc, (s * CH + tt * 128) * D + dvc * CH, [[D, 128], [1, CH]]),
                    in_=sb_[:, :]), reads=(bs,), writes=(b_vl,), inc_n=16)
        p.barrier()

    if upto == 1 or upto == 2:
        finish(p, es, nc)
        return nc

    if not L3:
      p.do("gpsimd", lambda e: e.memset(stats_sb[:, :], 0.0), writes=(b_stats,))
      p.do("vector", lambda e: e.tensor_copy(stats_sb[0:1, 0:1], scrow[0:1, 9:10]),
           reads=(b_scrow,), writes=(b_stats,))
      p.do("scalar", lambda e: e.dma_start(out=stats_loc.ap(), in_=stats_sb[:, :]),
           reads=(b_stats,), writes=(b_statl,), inc_n=16)
    if LA:
        finish(p, es, nc)
        return nc
    if FUSED:
        for (il, ol, bl) in ((kt_loc, kt_g, b_ktl), (v_loc, v_g, b_vl),
                             (stats_loc, stats_g, b_statl)):
            p.do("gpsimd", lambda e, il=il, ol=ol: e.collective_compute(
                "AllGather", ALU.bypass, replica_groups=[list(range(NCORE))],
                ins=[il.ap().opt()], outs=[ol.ap().opt()]),
                reads=(bl,), writes=(b_kvg,), inc_n=16, sem=s_cc)
        p.barrier(extra=(s_cc,))

    p.do("sync", lambda e: e.dma_start(out=cpos_sb[:, :], in_=cpos_t.ap()),
         writes=(b_cpos,), inc_n=16)
    p.do("sync", lambda e: e.dma_start(out=frow_sb[:, :], in_=frow_t.ap()),
         writes=(b_frow,), inc_n=16)
    p.do("sync", lambda e: e.dma_start(
        out=dview(ctx.ln_st, 0, [[4 * CH, 1], [1, NCORE * 128]]),
        in_=dview(stats_g, 0, [[NCORE * 128, 1], [1, NCORE * 128]])),
        reads=(b_kvg,), writes=(ctx.b_ln_st,), inc_n=16)
    p.do("vector", lambda e: e.reduce_max(scrow[0:1, 10:11], ctx.ln_st[0:1, 0:NCORE * 128], AXL.X),
         reads=(ctx.b_ln_st,), writes=(b_scrow,))

    jflat = [0]
    for s in range(NCH):
        p.do("sync", lambda e, s=s: e.dma_start(
            out=qtb[:, :], in_=dview(qt_hbm, s * 128 * COLS, [[COLS, 128], [1, COLS]])),
            reads=(b_qth,), writes=(b_qtb,), inc_n=16)
        p.do("vector", lambda e: e.tensor_tensor(pts[:, 0:COLS], qtb[:, :], qtb[:, :], ALU.mult),
             reads=(b_qtb,), writes=(b_pts,))
        mm_group(ctx, l_ps[0:1, 0:CH],
                 [(ctx.ones_col_bf[:, :], pts[:, dt * CH:(dt + 1) * CH]) for dt in range(DT)],
                 (b_pts, ctx.b_ones), b_lp)
        p.do("vector", lambda e: e.reduce_max(scrow[0:1, 11:12], l_ps[0:1, 0:CH], AXL.X),
             reads=(b_lp,), writes=(b_scrow,))
        emit_negm_from((10, 11))
        p.do("gpsimd", lambda e: e.memset(acc[:, :], 0.0), writes=(b_acc,))

        nseg_s = SLOT_SEGS[s]
        for r in range(nseg_s):
            first_seg = (r == 0)
            for u in range(SEGT):
                jf = jflat[0] + (r * SEGT + u)
                bu = u % 2
                t_glob = r * SEGT + u
                jk = t_glob // 4
                rk, sk = rank_slot_of_chunk(jk)
                kt_off = (rk * 2048 + (sk * 4 + t_glob % 4) * 128) * D
                v_off = (rk * 2048 + sk * CH + (t_glob % 4) * 128) * D
                p.do("scalar", lambda e, kt_off=kt_off, bu=bu: e.dma_start(
                    out=dview(ktile[bu], 0, [[DT * 128, 128], [1, DT * 128]]),
                    in_=dview(kt_g, kt_off, [[D, 128], [1, D]])),
                    reads=(b_kvg,), writes=(b_ktile[bu],), inc_n=16)
                p.do("sync", lambda e, v_off=v_off, u=u: e.dma_start(
                    out=dview(vbig, u * DT * 128, [[SEGT * DT * 128, 128], [1, DT * 128]]),
                    in_=dview(v_g, v_off, [[D, 128], [1, D]])),
                    reads=(b_kvg,), writes=(b_vbig,), inc_n=16)

                mm_group(ctx, mm_ps[bu][:, :],
                         [(ktile[bu][:, dt * 128:(dt + 1) * 128], qtb[:, dt * CH:(dt + 1) * CH])
                          for dt in range(DT)],
                         (b_ktile[bu], b_qtb), b_mm[bu])
                p.do("scalar", lambda e, u=u, bu=bu: e.activation(
                    pts[:, u * CH:(u + 1) * CH], mm_ps[bu][:, :], AF.Exp,
                    bias=negm[:, 0:1], scale=1.0),
                    reads=(b_mm[bu], b_negm), writes=(b_pts,))
                p.do("vector", lambda e, u=u, jf=jf: e.scalar_tensor_tensor(
                    pts[:, u * CH:(u + 1) * CH], frow_sb[:, :], cpos_sb[:, jf:jf + 1],
                    pts[:, u * CH:(u + 1) * CH], ALU.is_ge, ALU.mult),
                    reads=(b_frow, b_cpos), writes=(b_pts,))
                mm_group(ctx, l_ps[0:1, 0:CH],
                         [(ctx.ones_col_bf[:, :], pts[:, u * CH:(u + 1) * CH])],
                         (b_pts, ctx.b_ones), b_lp,
                         start=(first_seg and u == 0), stop=(r == nseg_s - 1 and u == SEGT - 1))
            for o in range(DT):
                bk = o % 2
                mm_group(ctx, av_ps[bk][:, :],
                         [(vbig[:, u2 * DT * 128 + o * 128: u2 * DT * 128 + o * 128 + 128],
                           pts[:, u2 * CH:(u2 + 1) * CH]) for u2 in range(SEGT)],
                         (b_vbig, b_pts), b_av[bk])
                p.do("vector", lambda e, o=o, bk=bk: e.tensor_tensor(
                    acc[:, o * CH:(o + 1) * CH], acc[:, o * CH:(o + 1) * CH],
                    av_ps[bk][:, :], ALU.add),
                    reads=(b_av[bk],), writes=(b_acc,))
        jflat[0] += nseg_s * SEGT

        p.do("vector", lambda e: e.reciprocal(linv[0:1, :], l_ps[0:1, 0:CH]),
             reads=(b_lp,), writes=(b_linv,))
        mm_group(ctx, l_ps[:, 0:CH], [(ctx.ones_row_f32[0:1, :], linv[0:1, :])],
                 (b_linv, ctx.b_ones), b_lp)
        p.do("vector", lambda e: e.tensor_copy(linvB[:, :], l_ps[:, 0:CH]),
             reads=(b_lp,), writes=(b_linvB,))
        for o in range(DT):
            p.do("vector", lambda e, o=o: e.tensor_tensor(
                pts[:, o * CH:(o + 1) * CH], acc[:, o * CH:(o + 1) * CH], linvB[:, :], ALU.mult),
                reads=(b_acc, b_linvB), writes=(b_pts,))
        for o in range(DT):
            buf = o % 2
            p.do("gpsimd", lambda e, o=o, buf=buf: e.dma_start(
                out=dview(wcol[buf], 0, [[DT * 128, 128], [128, DT], [1, 128]]),
                in_=dview(oproj_t, o * 128, [[D, 128], [128 * D, DT], [1, 128]])),
                writes=(b_wcol[buf],), inc_n=16)
            bk = o % 2
            mm_group(ctx, mm_ps[bk][:, :],
                     [(wcol[buf][:, g * 128:(g + 1) * 128], pts[:, g * CH:(g + 1) * CH])
                      for g in range(DT)],
                     (b_wcol[buf], b_pts), b_mm[bk])
            xb_, ub_ = xu_t[2 * (o % 2)], xu_t[2 * (o % 2) + 1]
            bxb, bub = b_xu[2 * (o % 2)], b_xu[2 * (o % 2) + 1]
            p.do("scalar", lambda e, s=s, o=o, xb_=xb_: e.dma_start(
                out=xb_[:, :], in_=dview(x2_hbm, s * 128 * COLS + o * CH, [[COLS, 128], [1, CH]])),
                reads=(b_x2h,), writes=(bxb,), inc_n=16)
            p.do("scalar", lambda e, s=s, o=o, ub_=ub_: e.dma_start(
                out=ub_[:, :], in_=dview(u_hbm, s * 128 * COLS + o * CH, [[COLS, 128], [1, CH]])),
                reads=(b_uh,), writes=(bub,), inc_n=16)
            st_ = x3stg[o % 2]
            bst = b_x3s[o % 2]
            p.do("vector", lambda e, bk=bk, xb_=xb_, st_=st_: e.tensor_tensor(
                st_[:, :], mm_ps[bk][:, :], xb_[:, :], ALU.add),
                reads=(b_mm[bk], bxb), writes=(bst,))
            p.do("vector", lambda e, ub_=ub_, st_=st_: e.tensor_tensor(
                st_[:, :], st_[:, :], ub_[:, :], ALU.add),
                reads=(bub,), writes=(bst,))
            p.do("scalar", lambda e, s=s, o=o, st_=st_: e.dma_start(
                out=dview(x3_hbm, s * 128 * COLS + o * CH, [[COLS, 128], [1, CH]]),
                in_=st_[:, :]), reads=(bst,), writes=(b_x3h,), inc_n=16)
        p.barrier()

    if upto == 3:
        finish(p, es, nc)
        return nc

    for s in range(NCH):
        p.do("sync", lambda e, s=s: e.dma_start(
            out=xT[:, :], in_=dview(x3_hbm, s * 128 * COLS, [[COLS, 128], [1, COLS]])),
            reads=(b_x3h,), writes=(b_xT,), inc_n=16)
        emit_ln(ctx, xT, b_xT, xn, b_xn)
        for it in range(IT1):
            gb, ub = it % 2, 2 + it % 2
            p.do("gpsimd", lambda e, it=it, gb=gb: e.dma_start(
                out=dview(wcol[gb], 0, [[DT * 128, 128], [128, DT], [1, 128]]),
                in_=dview(wg1, it * 128, [[INTER1, 128], [128 * INTER1, DT], [1, 128]])),
                writes=(b_wcol[gb],), inc_n=16)
            mm_group(ctx, mm_ps[0][:, :],
                     [(wcol[gb][:, dt * 128:(dt + 1) * 128], xn[:, dt * CH:(dt + 1) * CH])
                      for dt in range(DT)],
                     (b_wcol[gb], b_xn), b_mm[0])
            p.do("scalar", lambda e, it=it: e.activation(
                h1[:, it * CH:(it + 1) * CH], mm_ps[0][:, :], AF.Silu, bias=0.0,
                scale=sg1_sb[:, it:it + 1]),
                reads=(b_mm[0], b_sc1), writes=(b_h1,))
            p.do("gpsimd", lambda e, it=it, ub=ub: e.dma_start(
                out=dview(wcol[ub], 0, [[DT * 128, 128], [128, DT], [1, 128]]),
                in_=dview(wu1, it * 128, [[INTER1, 128], [128 * INTER1, DT], [1, 128]])),
                writes=(b_wcol[ub],), inc_n=16)
            mm_group(ctx, mm_ps[1][:, :],
                     [(wcol[ub][:, dt * 128:(dt + 1) * 128], xn[:, dt * CH:(dt + 1) * CH])
                      for dt in range(DT)],
                     (b_wcol[ub], b_xn), b_mm[1])
            p.do("vector", lambda e, it=it: e.scalar_tensor_tensor(
                h1[:, it * CH:(it + 1) * CH], mm_ps[1][:, :], su1_sb[:, it:it + 1],
                h1[:, it * CH:(it + 1) * CH], ALU.mult, ALU.mult),
                reads=(b_mm[1], b_sc1), writes=(b_h1,))
        for o in range(DT):
            db = o % 2
            p.do("gpsimd", lambda e, o=o, db=db: e.dma_start(
                out=dview(wdcolC[db], 0, [[IT1 * 128, 128], [128, IT1], [1, 128]]),
                in_=dview(wd1, o * 128, [[D, 128], [128 * D, IT1], [1, 128]])),
                writes=(b_wdC[db],), inc_n=16)
            bk = o % 2
            mm_group(ctx, mm_ps[bk][:, :],
                     [(wdcolC[db][:, ht * 128:(ht + 1) * 128], h1[:, ht * CH:(ht + 1) * CH])
                      for ht in range(IT1)],
                     (b_wdC[db], b_h1), b_mm[bk])
            p.do("vector", lambda e, o=o, bk=bk: e.scalar_tensor_tensor(
                xT[:, o * CH:(o + 1) * CH], mm_ps[bk][:, :], sd1_sb[:, o:o + 1],
                xT[:, o * CH:(o + 1) * CH], ALU.mult, ALU.add),
                reads=(b_mm[bk], b_sc1), writes=(b_xT,))
        for tt in range(4):
            for dt in range(DT):
                bk = dt % 2
                p.do("tensor", lambda e, tt=tt, dt=dt, bk=bk: e.transpose(
                    mm_ps[bk][:, 0:128],
                    xT[:, dt * CH + tt * 128: dt * CH + (tt + 1) * 128],
                    ctx.id_sb[:, :]),
                    reads=(b_xT, ctx.b_id), writes=(b_mm[bk],))
                p.do("vector", lambda e, dt=dt, bk=bk: e.tensor_copy(
                    xout[:, dt * 128:(dt + 1) * 128], mm_ps[bk][:, 0:128]),
                    reads=(b_mm[bk],), writes=(b_xout,))
            p.do("scalar", lambda e, s=s, tt=tt: e.dma_start(
                out=dview(out_rows, (s * CH + tt * 128) * D, [[D, 128], [1, D]]),
                in_=xout[:, :]), reads=(b_xout,), inc_n=16)
        p.barrier()

    finish(p, es, nc)
    return nc


def finish(p, es, nc):
    p.barrier()
    emit_block(nc, p)
    es.close()


IN_SHAPES = dict(
    x_in=((NCH, 2, CH, D), np.float32), wq=((D, D), np.float32),
    wk=((D, DKV), np.float32), wv=((D, DKV), np.float32), wo=((D, D), np.float32),
    masks0=((NCH, 8, 128, 512), BF), ident=((128, 128), np.float32),
    lut=((1, LUTN), np.float32), gidx=((D, INTER0), np.int32),
    uidx=((D, INTER0), np.int32), didx=((INTER0, D), np.int32),
    qkv=((D, 3 * D), np.float32), oproj=((D, D), np.float32),
    gam=((128, DT), np.float32), bet=((128, DT), np.float32),
    wg1=((D, INTER1), np.int8), wu1=((D, INTER1), np.int8),
    wd1=((INTER1, D), np.int8),
    sg1=((128, IT1), np.float32), su1=((128, IT1), np.float32),
    sd1=((128, DT), np.float32),
    cpos=((128, NTILE_B), np.float16), frow=((128, 512), np.float16),
    x2_hbm=((NCH, 128, COLS), np.float32), u_hbm=((NCH, 128, COLS), np.float32),
    qt_hbm=((NCH, 128, COLS), BF),
    kt_g=((NCORE * D, NCH * CH), BF), v_g=((NCORE * NCH * CH, D), BF),
    stats_g=((NCORE, 128), np.float32),
)


def _maps_for(nc, per_core):
    import concourse.mybir as mb
    names = []
    for alloc in nc.m.functions[0].allocations:
        if isinstance(alloc, mb.MemoryLocationSet) and alloc.kind == "ExternalInput":
            n = alloc.memorylocations[0].name
            if n in IN_SHAPES:
                names.append(n)
    maps = []
    for core in range(NCORE):
        m = {}
        for n in names:
            if n in per_core[core]:
                m[n] = per_core[core][n]
            else:
                shape, dt = IN_SHAPES[n]
                m[n] = np.zeros(shape, dt)
        maps.append(m)
    return maps


def kernel(**inputs):
    import os
    x = np.asarray(inputs["L0_attn_x"]).reshape(T, D)
    wq = np.ascontiguousarray(inputs["L0_attn_Wq"], dtype=np.float32)
    wk = np.ascontiguousarray(inputs["L0_attn_Wk"], dtype=np.float32)
    wv = np.ascontiguousarray(inputs["L0_attn_Wv"], dtype=np.float32)
    wo = np.ascontiguousarray(inputs["L0_attn_Wo"], dtype=np.float32)
    lut = np.ascontiguousarray(inputs["L0_ffn_lut"], dtype=np.float32).reshape(1, LUTN)
    gidx_full = np.asarray(inputs["L0_ffn_gate_idx"], dtype=np.int32)
    uidx_full = np.asarray(inputs["L0_ffn_up_idx"], dtype=np.int32)
    didx_full = np.asarray(inputs["L0_ffn_down_idx"], dtype=np.int32)
    qkv = np.ascontiguousarray(inputs["L1_attn_qkv"], dtype=np.float32)
    oproj = np.ascontiguousarray(inputs["L1_attn_o_proj"], dtype=np.float32)
    gam = np.ascontiguousarray(np.asarray(inputs["L1_attn_gamma"], dtype=np.float32).reshape(DT, 128).T)
    bet = np.ascontiguousarray(np.asarray(inputs["L1_attn_beta"], dtype=np.float32).reshape(DT, 128).T)
    wg1 = np.ascontiguousarray(inputs["L1_ffn_w_gate"])
    wu1 = np.ascontiguousarray(inputs["L1_ffn_w_up"])
    wd1 = np.ascontiguousarray(inputs["L1_ffn_w_down"])
    sg1 = np.ascontiguousarray(np.asarray(inputs["L1_ffn_s_gate"], dtype=np.float32).reshape(IT1, 128).T)
    su1 = np.ascontiguousarray(np.asarray(inputs["L1_ffn_s_up"], dtype=np.float32).reshape(IT1, 128).T)
    sd1 = np.ascontiguousarray(np.asarray(inputs["L1_ffn_s_down"], dtype=np.float32).reshape(DT, 128).T)
    ident = np.eye(128, dtype=np.float32)

    xsl = []
    for core in range(NCORE):
        a = np.zeros((NCH, 2, CH, D), dtype=np.float32)
        for s in range(NCH):
            j = chunk_of_slot(core, s)
            a[s, 1] = x[j * CH:(j + 1) * CH]
            if j > 0:
                a[s, 0] = x[(j - 1) * CH:j * CH]
        xsl.append(a)

    mode = os.environ.get("MK_MODE", "split")
    cores = list(range(NCORE))
    if mode == "fused":
        upto = int(os.environ.get("MK_UPTO", "4"))
        nc = build(upto=upto, mode="fused")
        per_core = [dict(
            x_in=xsl[c], wq=wq, wk=wk, wv=wv, wo=wo,
            masks0=host_masks_l0(c), ident=ident, lut=lut,
            gidx=np.ascontiguousarray(gidx_full.T),
            uidx=np.ascontiguousarray(uidx_full.T),
            didx=np.ascontiguousarray(didx_full.T),
            qkv=qkv, oproj=oproj, gam=gam, bet=bet,
            wg1=wg1, wu1=wu1, wd1=wd1, sg1=sg1, su1=su1, sd1=sd1,
            cpos=host_cpos(c), frow=host_frow()) for c in cores]
        res = run_bass_kernel_spmd(nc, _maps_for(nc, per_core), core_ids=cores)
        if upto >= 4:
            out = np.zeros((T, D), dtype=np.float32)
            for c in cores:
                r = res.results[c]["out_rows"]
                for s in range(NCH):
                    j = chunk_of_slot(c, s)
                    out[j * CH:(j + 1) * CH] = r[s]
            return out.reshape(1, T, D)
        key = {1: "x1_hbm", 2: "x2_hbm", 3: "x3_hbm"}[upto]
        out = np.zeros((T, D), dtype=np.float32)
        for c in cores:
            r = res.results[c][key]
            for s in range(NCH):
                j = chunk_of_slot(c, s)
                t = r[s].reshape(128, DT, CH)
                out[j * CH:(j + 1) * CH] = t.transpose(2, 1, 0).reshape(CH, D)
        return out.reshape(1, T, D)

    gidxT = np.ascontiguousarray(gidx_full.T)
    uidxT = np.ascontiguousarray(uidx_full.T)
    didxT = np.ascontiguousarray(didx_full.T)
    ncA = build(mode="LA")
    pcA = [dict(
        x_in=xsl[c], wq=wq, wk=wk, wv=wv, wo=wo,
        masks0=host_masks_l0(c), ident=ident, lut=lut,
        gidx=gidxT, uidx=uidxT, didx=didxT,
        qkv=qkv, gam=gam, bet=bet,
    ) for c in cores]
    rA = run_bass_kernel_spmd(ncA, _maps_for(ncA, pcA), core_ids=cores).results
    sup = int(os.environ.get("MK_SPLIT_UPTO", "3"))

    def asm_t(key, rr):
        o = np.zeros((T, D), dtype=np.float32)
        for c in cores:
            r = rr[c][key]
            for s in range(NCH):
                j = chunk_of_slot(c, s)
                t = r[s].reshape(128, DT, CH)
                o[j * CH:(j + 1) * CH] = t.transpose(2, 1, 0).reshape(CH, D)
        return o.reshape(1, T, D)

    if sup == 2:
        return asm_t("x2_hbm", rA)
    kt_g = np.concatenate([rA[c]["kt_loc"] for c in cores], axis=0)
    v_g = np.concatenate([rA[c]["v_loc"] for c in cores], axis=0)
    stats_g = np.concatenate([rA[c]["stats_loc"] for c in cores], axis=0)

    nc3 = build(mode="L3")
    pc3 = [dict(
        x2_hbm=rA[c]["x2_hbm"], u_hbm=rA[c]["u_hbm"], qt_hbm=rA[c]["qt_hbm"],
        kt_g=kt_g, v_g=v_g, stats_g=stats_g,
        oproj=oproj, ident=ident,
        wg1=wg1, wu1=wu1, wd1=wd1, sg1=sg1, su1=su1, sd1=sd1,
        cpos=host_cpos(c), frow=host_frow(),
    ) for c in cores]
    r3 = run_bass_kernel_spmd(nc3, _maps_for(nc3, pc3), core_ids=cores).results
    out = np.zeros((T, D), dtype=np.float32)
    for c in cores:
        r = r3[c]["out_rows"]
        for s in range(NCH):
            j = chunk_of_slot(c, s)
            out[j * CH:(j + 1) * CH] = r[s]
    return out.reshape(1, T, D)


# version 03 (final)
# speedup vs baseline: 2.0742x; 2.0742x over previous
import numpy as np
import ml_dtypes
from contextlib import ExitStack

import concourse.bass as bass
import concourse.mybir as mybir
from concourse.bass_utils import run_bass_kernel_spmd

F32 = mybir.dt.float32
BF16 = mybir.dt.bfloat16
I32 = mybir.dt.int32
I8 = mybir.dt.int8
F16 = mybir.dt.float16
AF = mybir.ActivationFunctionType
ALU = mybir.AluOpType
AXL = mybir.AxisListType
BF = ml_dtypes.bfloat16

T, D, NCORE, NCH, CH = 16384, 2048, 8, 4, 512
DT = D // 128
COLS = DT * CH
H, HK, HD = 16, 4, 128
DKV = HK * HD
INTER0, LUTN = 5504, 16
IT0 = INTER0 // 128
INTER1 = 8192
IT1 = INTER1 // 128
EPS = 1e-5
SCALE0 = 1.0 / np.sqrt(HD)
SCALE1 = 1.0 / np.sqrt(2048.0)
SEG = 2048
SEGT = SEG // 128
SLOT_SEGS = [8, 6, 4, 2]
NSEG = sum(SLOT_SEGS)
NTILE_B = NSEG * SEGT
DEC = D // NCORE


def chunk_of_slot(core, s):
    return (31 - 8 * s) - core


def rank_slot_of_chunk(j):
    return (7 - j) % 8, 3 - j // 8


def host_masks_l0(core):
    p = np.arange(128)[:, None]
    f = np.arange(512)[None, :]
    m = np.zeros((NCH, 8, 128, 512), dtype=BF)
    for s in range(NCH):
        jg = chunk_of_slot(core, s)
        for j in range(8):
            if j < 4:
                m[s, j] = 0.0 if jg == 0 else (f < j * 128 + p).astype(BF)
            else:
                m[s, j] = ((j - 4) * 128 + p <= f).astype(BF)
    return m


def host_maskbank():
    p = np.arange(128)[:, None]
    f = np.arange(512)[None, :]
    b = np.zeros((6, 128, 512), dtype=BF)
    b[0] = 1.0
    b[1] = 0.0
    for d in range(4):
        b[2 + d] = (p + d * 128 <= f).astype(BF)
    return b


def host_cpos(core):
    p = np.arange(128)[:, None]
    cols = []
    for s in range(NCH):
        jq = chunk_of_slot(core, s)
        for r in range(SLOT_SEGS[s]):
            for u in range(SEGT):
                t = r * SEGT + u
                cols.append(np.clip(t * 128 + p - jq * CH, -1024, 1024))
    return np.clip(np.concatenate(cols, axis=1), -1024, 1024).astype(np.float16)


def host_frow():
    return np.broadcast_to(np.arange(512, dtype=np.float16)[None, :], (128, 512)).copy()


ENGINES = ("sync", "gpsimd", "tensor", "vector", "scalar")


class Sem:
    __slots__ = ("h", "val", "name")

    def __init__(self, h, name):
        self.h, self.val, self.name = h, 0, name


class Buf:
    __slots__ = ("name", "prod", "reads", "dram", "dsem")

    def __init__(self, name, dram=False):
        self.name, self.prod, self.reads = name, None, []
        self.dram, self.dsem = dram, {}


class Prog:
    def __init__(self):
        self.q = {e: [] for e in ENGINES}
        self.lastw = {e: {} for e in ENGINES}
        self.esem = {}
        self.dsem = {}
        self.mk_sem = None
        self.bufsems = []

    def wait(self, eng, sem, val):
        if sem is None or val is None or val <= 0:
            return
        lw = self.lastw[eng]
        if lw.get(sem.name, -1) >= val:
            return
        lw[sem.name] = val
        h = sem.h
        self.q[eng].append(lambda e, h=h, v=val: e.wait_ge(h, v))

    def do(self, eng, fn, reads=(), writes=(), inc_n=1, sem=None):
        for b in reads:
            if b.prod:
                self.wait(eng, *b.prod)
        for b in writes:
            if b.prod:
                self.wait(eng, *b.prod)
            for r in b.reads:
                self.wait(eng, *r)
        if sem is None and inc_n == 16:
            qt = "sw" if eng == "gpsimd" else "hw"
            cand = [b for b in list(writes) + list(reads) if not b.dram]
            if cand:
                b0 = cand[0]
                if qt not in b0.dsem:
                    b0.dsem[qt] = self.mk_sem(f"bds_{qt}_{b0.name}")
                    self.bufsems.append(b0.dsem[qt])
                sem = b0.dsem[qt]
            else:
                sem = self.dsem[eng]
        s = sem or (self.dsem[eng] if inc_n == 16 else self.esem[eng])
        tgt = s.val + inc_n
        s.val = tgt
        h = s.h

        def wrapped(e, fn=fn, h=h, n=inc_n):
            fn(e).then_inc(h, n)
        self.q[eng].append(wrapped)
        edge = (s, tgt)
        for b in writes:
            b.prod = edge
            b.reads = []
        for b in reads:
            b.reads.append(edge)
        return edge

    def barrier(self, extra=(), skip=(), skip_sems=()):
        sems = (list(self.esem.values()) + list(self.dsem.values())
                + self.bufsems + list(extra))
        for e in ENGINES:
            if e in skip:
                continue
            for s in sems:
                if s in skip_sems:
                    continue
                self.wait(e, s, s.val)


def emit_block(nc, prog):
    with nc.Block() as block:
        @block.sync
        def _(e):
            for f in prog.q["sync"]:
                f(e)

        @block.gpsimd
        def _(e):
            for f in prog.q["gpsimd"]:
                f(e)

        @block.tensor
        def _(e):
            for f in prog.q["tensor"]:
                f(e)

        @block.vector
        def _(e):
            for f in prog.q["vector"]:
                f(e)

        @block.scalar
        def _(e):
            for f in prog.q["scalar"]:
                f(e)


def dview(t, off, dims):
    return bass.AP(t, off, [list(d) for d in dims])


DTBYTES = {F32: 4, BF16: 2, I32: 4, I8: 1, F16: 2}


class Mem:
    def __init__(self, nc, base, top):
        self.nc, self.off, self.top = nc, base, top

    def take(self, name, shape, dtype):
        nb = (shape[-1] * DTBYTES[dtype] + 31) // 32 * 32
        assert self.off + nb <= self.top, f"SBUF overflow at {name}: {self.off}+{nb}"
        t = self.nc.alloc_sbuf_tensor_at(name, list(shape), dtype, offset=self.off)
        self.off += nb
        return t

    def mark(self):
        return self.off

    def reset(self, off):
        self.off = off


class Ctx:
    pass


def mm_group(ctx, psum_ap, terms, reads, wbuf, start=True, stop=True):
    n = len(terms)

    def fn(e):
        last = None
        for i, (lt, rt) in enumerate(terms):
            last = e.matmul(psum_ap, lt, rt, start=(start and i == 0),
                            stop=(stop and i == n - 1), skip_group_check=True)
        return last
    return ctx.p.do("tensor", fn, reads=reads, writes=(wbuf,))


def emit_ln(ctx, xT_sb, b_xT, out_xn, b_xn, gamma=None, beta=None, b_gb=None,
            out_u=None, b_u=None):
    p, c = ctx.p, ctx
    xb, b_xb = c.ln_xb, c.b_ln_xb
    st, b_st = c.ln_st, c.b_ln_st
    aB, b_aB = c.ln_aB, c.b_ln_aB
    bB, b_bB = c.ln_bB, c.b_ln_bB
    scr, b_scr = c.ln_scr, c.b_ln_scr
    ms, b_ms = c.ms_ps, c.b_ms

    p.do("vector", lambda e: e.tensor_copy(xb[:, :], xT_sb[:, :]), reads=(b_xT,), writes=(b_xb,))
    mm_group(ctx, ms[0:1, 0:CH],
             [(c.ones_col_bf[:, :], xb[:, dt * CH:(dt + 1) * CH]) for dt in range(DT)],
             (b_xb, c.b_ones), b_ms)
    p.do("vector", lambda e: e.tensor_copy(st[0:1, 0:CH], ms[0:1, 0:CH]), reads=(b_ms,), writes=(b_st,))
    p.do("vector", lambda e: e.tensor_tensor(xb[:, :], xb[:, :], xb[:, :], ALU.mult),
         reads=(b_xb,), writes=(b_xb,))
    mm_group(ctx, ms[0:1, 0:CH],
             [(c.ones_col_bf[:, :], xb[:, dt * CH:(dt + 1) * CH]) for dt in range(DT)],
             (b_xb, c.b_ones), b_ms)
    p.do("vector", lambda e: e.tensor_copy(st[0:1, CH:2 * CH], ms[0:1, 0:CH]), reads=(b_ms,), writes=(b_st,))
    p.do("vector", lambda e: e.tensor_scalar_mul(st[0:1, 0:CH], st[0:1, 0:CH], 1.0 / D), reads=(b_st,), writes=(b_st,))
    p.do("vector", lambda e: e.tensor_scalar_mul(st[0:1, CH:2 * CH], st[0:1, CH:2 * CH], 1.0 / D), reads=(b_st,), writes=(b_st,))
    p.do("vector", lambda e: e.tensor_tensor(st[0:1, 2 * CH:3 * CH], st[0:1, 0:CH], st[0:1, 0:CH], ALU.mult), reads=(b_st,), writes=(b_st,))
    p.do("vector", lambda e: e.tensor_tensor(st[0:1, CH:2 * CH], st[0:1, CH:2 * CH], st[0:1, 2 * CH:3 * CH], ALU.subtract), reads=(b_st,), writes=(b_st,))
    p.do("vector", lambda e: e.tensor_scalar_add(st[0:1, CH:2 * CH], st[0:1, CH:2 * CH], float(EPS)), reads=(b_st,), writes=(b_st,))
    p.do("scalar", lambda e: e.activation(st[0:1, 2 * CH:3 * CH], st[0:1, CH:2 * CH], AF.Ln, bias=0.0, scale=1.0), reads=(b_st,), writes=(b_st,))
    p.do("scalar", lambda e: e.activation(st[0:1, 2 * CH:3 * CH], st[0:1, 2 * CH:3 * CH], AF.Exp, bias=0.0, scale=-0.5), reads=(b_st,), writes=(b_st,))
    p.do("vector", lambda e: e.tensor_tensor(st[0:1, 3 * CH:4 * CH], st[0:1, 0:CH], st[0:1, 2 * CH:3 * CH], ALU.mult), reads=(b_st,), writes=(b_st,))
    p.do("vector", lambda e: e.tensor_scalar_mul(st[0:1, 3 * CH:4 * CH], st[0:1, 3 * CH:4 * CH], -1.0), reads=(b_st,), writes=(b_st,))
    mm_group(ctx, ms[:, 0:CH], [(c.ones_row_f32[0:1, :], st[0:1, 2 * CH:3 * CH])], (b_st, c.b_ones), b_ms)
    p.do("vector", lambda e: e.tensor_copy(aB[:, :], ms[:, 0:CH]), reads=(b_ms,), writes=(b_aB,))
    mm_group(ctx, ms[:, 0:CH], [(c.ones_row_f32[0:1, :], st[0:1, 3 * CH:4 * CH])], (b_st, c.b_ones), b_ms)
    p.do("vector", lambda e: e.tensor_copy(bB[:, :], ms[:, 0:CH]), reads=(b_ms,), writes=(b_bB,))
    for dt in range(DT):
        c0, c1 = dt * CH, (dt + 1) * CH
        if out_u is not None:
            p.do("vector", lambda e, c0=c0, c1=c1: e.tensor_tensor(out_u[:, c0:c1], xT_sb[:, c0:c1], aB[:, :], ALU.mult),
                 reads=(b_xT, b_aB), writes=(b_u,))
            p.do("vector", lambda e, c0=c0, c1=c1: e.tensor_tensor(out_u[:, c0:c1], out_u[:, c0:c1], bB[:, :], ALU.add),
                 reads=(b_bB,), writes=(b_u,))
        elif gamma is None:
            p.do("vector", lambda e, c0=c0, c1=c1: e.tensor_tensor(scr[:, 0:CH], xT_sb[:, c0:c1], aB[:, :], ALU.mult),
                 reads=(b_xT, b_aB), writes=(b_scr,))
            p.do("vector", lambda e, c0=c0, c1=c1: e.tensor_tensor(out_xn[:, c0:c1], scr[:, 0:CH], bB[:, :], ALU.add),
                 reads=(b_scr, b_bB), writes=(b_xn,))
        else:
            p.do("vector", lambda e, c0=c0, c1=c1: e.tensor_tensor(scr[:, 0:CH], xT_sb[:, c0:c1], aB[:, :], ALU.mult),
                 reads=(b_xT, b_aB), writes=(b_scr,))
            p.do("vector", lambda e, c0=c0, c1=c1: e.tensor_tensor(scr[:, 0:CH], scr[:, 0:CH], bB[:, :], ALU.add),
                 reads=(b_bB,), writes=(b_scr,))
            p.do("vector", lambda e, c0=c0, c1=c1, dt=dt: e.tensor_scalar(
                out_xn[:, c0:c1], scr[:, 0:CH], gamma[:, dt:dt + 1], beta[:, dt:dt + 1], ALU.mult, ALU.add),
                reads=(b_scr, b_gb), writes=(b_xn,))


def build(upto=4, mode="fused"):
    LA = mode == "LA"
    L1 = mode == "L1"
    L2 = mode == "L2"
    L3 = mode == "L3"
    FUSED = mode == "fused"
    A_ANY = FUSED or LA or L1 or L2
    PASS1 = FUSED or LA or L1
    PASS2 = FUSED or LA or L2
    DECROWS_GU = D if not L1 else 2 * 128
    DECROWS_D = INTER0 if not L1 else 6 * 128
    nc = bass.Bass("TRN2", target_bir_lowering=False)
    es = ExitStack()
    ctx = Ctx()
    ctx.nc = nc
    p = Prog()
    ctx.p = p
    for e in ENGINES:
        p.esem[e] = Sem(es.enter_context(nc.semaphore("sem_" + e)), "sem_" + e)
        p.dsem[e] = Sem(es.enter_context(nc.semaphore("dma_" + e)), "dma_" + e)
    p.mk_sem = lambda nm: Sem(es.enter_context(nc.semaphore(nm)), nm)
    s_cc = Sem(es.enter_context(nc.semaphore("sem_cc")), "sem_cc")

    DI = lambda n, s, d: nc.dram_tensor(n, s, d, kind="ExternalInput")
    DN = lambda n, s, d: nc.dram_tensor(n, s, d, kind="Internal")
    DO = lambda n, s, d: nc.dram_tensor(n, s, d, kind="ExternalOutput")
    x_in = DI("x_in", [NCH, 2, CH, D], F32)
    wq = DI("wq", [D, D], F32)
    wk = DI("wk", [D, DKV], F32)
    wv = DI("wv", [D, DKV], F32)
    wo = DI("wo", [D, D], F32)
    masks0 = DI("masks0", [NCH, 8, 128, 512], BF16)
    ident = DI("ident", [128, 128], F32)
    lut_t = DI("lut", [1, LUTN], F32)
    gidx = DI("gidx", [DECROWS_GU, INTER0], I32)
    uidx = DI("uidx", [DECROWS_GU, INTER0], I32)
    didx = DI("didx", [DECROWS_D, D], I32)
    qkv_t = DI("qkv", [D, 3 * D], F32)
    oproj_t = DI("oproj", [D, D], F32)
    gam_t = DI("gam", [128, DT], F32)
    bet_t = DI("bet", [128, DT], F32)
    wg1 = DI("wg1", [D, INTER1], I8)
    wu1 = DI("wu1", [D, INTER1], I8)
    wd1 = DI("wd1", [INTER1, D], I8)
    sg1_t = DI("sg1", [128, IT1], F32)
    su1_t = DI("su1", [128, IT1], F32)
    sd1_t = DI("sd1", [128, DT], F32)
    cpos_t = DI("cpos", [128, NTILE_B], F16)
    frow_t = DI("frow", [128, 512], F16)

    DWK = DO if L1 else DN
    gwt_loc = DWK("gwt_loc", [DECROWS_GU, INTER0], BF16) if PASS1 else None
    uwt_loc = DWK("uwt_loc", [DECROWS_GU, INTER0], BF16) if PASS1 else None
    dwt_loc = DWK("dwt_loc", [DECROWS_D, D], BF16) if PASS1 else None
    gwt_g = DI("gwt_g", [D, INTER0], BF16) if L2 else gwt_loc
    uwt_g = DI("uwt_g", [D, INTER0], BF16) if L2 else uwt_loc
    dwt_g = DI("dwt_g", [INTER0, D], BF16) if L2 else dwt_loc
    b_gwtg = Buf("gwtg", 1)
    kt_loc = (DO if (LA or L2) else DN)("kt_loc", [D, NCH * CH], BF16) if (FUSED or LA or L2) else None
    v_loc = (DO if (LA or L2) else DN)("v_loc", [NCH * CH, D], BF16) if (FUSED or LA or L2) else None
    stats_loc = (DO if (LA or L2) else DN)("stats_loc", [1, 128], F32) if (FUSED or LA or L2) else None
    kt_g = (DI if L3 else DN)("kt_g", [NCORE * D, NCH * CH], BF16) if (FUSED or L3) else None
    v_g = (DI if L3 else DN)("v_g", [NCORE * NCH * CH, D], BF16) if (FUSED or L3) else None
    stats_g = (DI if L3 else DN)("stats_g", [NCORE, 128], F32) if (FUSED or L3) else None
    u_hbm = (DO if (LA or L2) else (DI if L3 else DN))("u_hbm", [NCH, 128, COLS], F32)
    qt_hbm = (DO if (LA or L2) else (DI if L3 else DN))("qt_hbm", [NCH, 128, COLS], BF16)
    x1_hbm = ((DO if (upto == 1 or L1) else (DI if L2 else DN))("x1_hbm", [NCH, 128, COLS], F32)
              if (A_ANY or upto == 1) else None)
    x2_hbm = (DO if (upto == 2 or LA or L2) else (DI if L3 else DN))("x2_hbm", [NCH, 128, COLS], F32)
    x3_hbm = ((DO if upto == 3 else DN)("x3_hbm", [NCH, 128, COLS], F32)
              if (upto >= 3 and not LA) else None)
    out_rows = DO("out_rows", [NCH, CH, D], F32) if (upto >= 4 and not LA) else None

    b_gwtl, b_uwtl, b_dwtl = Buf("gwtl", 1), Buf("uwtl", 1), Buf("dwtl", 1)
    b_wtg = Buf("wtg", 1)
    b_ktl, b_vl, b_statl = Buf("ktl", 1), Buf("vl", 1), Buf("statl", 1)
    b_kvg = Buf("kvg", 1)
    b_uh, b_qth, b_x2h, b_x3h = Buf("uh", 1), Buf("qth", 1), Buf("x2h", 1), Buf("x3h", 1)
    b_x1h = Buf("x1h", 1)

    mem = Mem(nc, 16512, 229344)
    tk = mem.take
    ctx.ones_col_bf = tk("ones_col_bf", [128, 1], BF16)
    ctx.ones_row_f32 = tk("ones_row_f32", [1, 128], F32)
    ctx.id_sb = tk("id_sb", [128, 128], F32)
    lut_b = tk("lut_b", [128, LUTN], F32)
    gam_sb = tk("gam_sb", [128, DT], F32)
    bet_sb = tk("bet_sb", [128, DT], F32)
    sg1_sb = tk("sg1_sb", [128, IT1], F32)
    su1_sb = tk("su1_sb", [128, IT1], F32)
    sd1_sb = tk("sd1_sb", [128, DT], F32)
    lutrow = tk("lutrow", [1, LUTN], F32)
    scrow = tk("scrow", [1, 64], F32)
    linv = tk("linv", [1, CH], F32)
    linvB = tk("linvB", [128, CH], F32)
    negm = tk("negm", [128, 1], F32)
    stats_sb = tk("stats_sb", [1, 128], F32)
    stgb = [tk(f"stgb{i}", [128, CH], BF16) for i in range(2)]
    wcol = [tk(f"wcol{i}", [128, DT * 128], BF16) for i in range(4)]
    ctx.ln_xb = tk("ln_xb", [128, COLS], BF16)
    ctx.ln_st = tk("ln_st", [1, 4 * CH], F32)
    ctx.ln_aB = tk("ln_aB", [128, CH], F32)
    ctx.ln_bB = tk("ln_bB", [128, CH], F32)
    ctx.ln_scr = tk("ln_scr", [128, CH], F32)
    for nm in ("ln_xb", "ln_st", "ln_aB", "ln_bB", "ln_scr"):
        setattr(ctx, "b_" + nm, Buf(nm))

    arena = mem.mark()
    xT = tk("xT", [128, COLS], F32)
    xn = tk("xn", [128, COLS], BF16)
    xin_sb = tk("xin_sb", [128, D], F32)
    solid_end = mem.mark()
    wv_sb = tk("wv_sb", [128, DT * DKV], BF16)
    kt_prev = tk("kt_prev", [128, HK * CH], BF16)
    kt_cur = tk("kt_cur", [128, HK * CH], BF16)
    v_prev = tk("v_prev", [128, 4 * DKV], BF16)
    v_cur = tk("v_cur", [128, 4 * DKV], BF16)
    qt0 = tk("qt0", [128, H * CH], BF16)
    ptb = tk("ptb", [128, 8 * CH], BF16)
    mask8 = tk("mask8", [128, 8 * CH], BF16)
    attn_out = tk("attn_out", [128, H * CH], BF16)
    DW = 1024
    DWP = 512
    dec_v = [tk(f"dec_v{i}", [128, DW], BF16) for i in range(4)]
    dec_p = [tk(f"dec_p{i}", [128, DWP], BF16) for i in range(4)]
    g1_end = mem.mark()
    mem.reset(solid_end)
    h0 = tk("h0", [128, IT0 * CH], BF16)
    wdcolA = [tk(f"wdcolA{i}", [128, IT0 * 128], BF16) for i in range(2)]
    g2_end = mem.mark()
    gd_end = mem.mark()
    mem.reset(solid_end)
    u_sb = tk("u_sb", [128, COLS], F32)
    vslab = tk("vslab", [128, DT * CH], BF16)
    g3_end = mem.mark()
    mem.reset(arena)
    qtb = tk("qtb", [128, COLS], BF16)
    pts = tk("pts", [128, SEGT * CH], BF16)
    acc = tk("acc", [128, DT * CH], F32)
    vbig = tk("vbig", [128, SEGT * DT * 128], BF16)
    ktile = [tk(f"ktile{i}", [128, DT * 128], BF16) for i in range(2)]
    cpos_sb = tk("cpos_sb", [128, NTILE_B], F16)
    frow_sb = tk("frow_sb", [128, 512], F16)
    xu_t = [tk(f"xu_t{i}", [128, CH], F32) for i in range(4)]
    x3stg = [tk(f"x3stg{i}", [128, CH], F32) for i in range(2)]
    b_end = mem.mark()
    mem.reset(solid_end)
    h1 = tk("h1", [128, IT1 * CH], BF16)
    wdcolC = [tk(f"wdcolC{i}", [128, IT1 * 128], BF16) for i in range(2)]
    xout = xin_sb
    c_end = mem.mark()
    assert max(g1_end, g2_end, gd_end, g3_end, b_end, c_end) <= 229344, \
        (g1_end, g2_end, gd_end, g3_end, b_end, c_end)

    pm = lambda n: es.enter_context(nc.psum_tensor(n, [128, CH], F32))
    mm_ps = [pm("mm_ps0"), pm("mm_ps1")]
    av_ps = [pm("av_ps0"), pm("av_ps1")]
    ctx.ms_ps = pm("ms_ps")
    l_ps = pm("l_ps")
    tpb_ps = es.enter_context(nc.psum_tensor("tpb_ps", [128, 128], BF16))

    ctx.b_ones, ctx.b_id, ctx.b_ms = Buf("ones"), Buf("id"), Buf("ms")
    b_idbf = Buf("idbf")
    b_lut, b_gb, b_sc1 = Buf("lut"), Buf("gb"), Buf("sc1")
    b_scrow, b_linv, b_linvB, b_negm = Buf("scrow"), Buf("linv"), Buf("linvB"), Buf("negm")
    b_cpos, b_frow, b_stats = Buf("cpos"), Buf("frow"), Buf("stats")
    b_stgb = [Buf("stgb0"), Buf("stgb1")]
    b_wcol = [Buf(f"wcol{i}") for i in range(4)]
    b_xT, b_xn, b_xin = Buf("xT"), Buf("xn"), Buf("xin")
    b_wv, b_ktp, b_ktc, b_vp, b_vc = Buf("wv"), Buf("ktp"), Buf("ktc"), Buf("vp"), Buf("vc")
    b_qt0, b_ptb, b_m8, b_ao = Buf("qt0"), Buf("ptb"), Buf("m8"), Buf("ao")
    b_h0 = Buf("h0")
    b_wdA = [Buf("wdA0"), Buf("wdA1")]
    b_dec_v = [Buf(f"dec_v{i}") for i in range(4)]
    b_dec_p = [Buf(f"dec_p{i}") for i in range(4)]
    b_u, b_vslab = Buf("u"), Buf("vslab")
    b_qtb, b_pts, b_acc, b_vbig = Buf("qtb"), Buf("pts"), Buf("acc"), Buf("vbig")
    b_ktile = [Buf("ktile0"), Buf("ktile1")]
    b_xu = [Buf(f"xu{i}") for i in range(4)]
    b_x3s = [Buf("x3s0"), Buf("x3s1")]
    b_h1 = Buf("h1")
    b_xout = None
    b_wdC = [Buf("wdC0"), Buf("wdC1")]
    b_xout = b_xin
    b_mm = [Buf("mm0"), Buf("mm1")]
    b_av = [Buf("av0"), Buf("av1")]
    b_lp, b_tpb = Buf("lp"), Buf("tpb")

    p.do("gpsimd", lambda e: e.memset(ctx.ones_col_bf[:, :], 1.0), writes=(ctx.b_ones,))
    p.do("gpsimd", lambda e: e.memset(ctx.ones_row_f32[:, :], 1.0), writes=(ctx.b_ones,))
    p.do("sync", lambda e: e.dma_start(out=ctx.id_sb[:, :], in_=ident.ap()), writes=(ctx.b_id,), inc_n=16)
    p.do("sync", lambda e: e.dma_start(out=gam_sb[:, :], in_=gam_t.ap()), writes=(b_gb,), inc_n=16)
    p.do("sync", lambda e: e.dma_start(out=bet_sb[:, :], in_=bet_t.ap()), writes=(b_gb,), inc_n=16)
    p.do("sync", lambda e: e.dma_start(out=sg1_sb[:, :], in_=sg1_t.ap()), writes=(b_sc1,), inc_n=16)
    p.do("sync", lambda e: e.dma_start(out=su1_sb[:, :], in_=su1_t.ap()), writes=(b_sc1,), inc_n=16)
    p.do("sync", lambda e: e.dma_start(out=sd1_sb[:, :], in_=sd1_t.ap()), writes=(b_sc1,), inc_n=16)
    p.do("sync", lambda e: e.dma_start(out=lutrow[0:1, :], in_=lut_t.ap()), writes=(b_lut,), inc_n=16)
    mm_group(ctx, ctx.ms_ps[:, 0:LUTN], [(ctx.ones_row_f32[0:1, :], lutrow[0:1, :])],
             (b_lut, ctx.b_ones), ctx.b_ms)
    p.do("vector", lambda e: e.tensor_copy(lut_b[:, :], ctx.ms_ps[:, 0:LUTN]), reads=(ctx.b_ms,), writes=(b_lut,))


    decode_units = []
    dsem_v = p.mk_sem("dec_comp")
    dsem_ld = [p.mk_sem("dec_ld0"), p.mk_sem("dec_ld1")]
    dsem_st = p.mk_sem("dec_st")

    def mk_unit(idx_t, out_t, b_out, ht, c0, w, eng, k):
        bufs = b_dec_v
        tiles = dec_v
        ib = k % 2
        ncols = INTER0 if out_t is gwt_loc or out_t is uwt_loc else D

        def emit():
            p.do("gpsimd", lambda e: e.dma_start(
                out=tiles[ib][:, 0:w],
                in_=dview(idx_t, ht * 128 * ncols + c0, [[ncols, 128], [1, w]])),
                writes=(bufs[ib],), inc_n=16, sem=dsem_ld[ib])
            for t in range(LUTN):
                if t == 0:
                    p.do(eng, lambda e: e.tensor_scalar(
                        tiles[2][:, 0:w], tiles[ib][:, 0:w], 0.0, lut_b[:, 0:1],
                        ALU.is_equal, ALU.mult),
                        reads=(bufs[ib], b_lut), writes=(bufs[2],), sem=dsem_v)
                else:
                    p.do(eng, lambda e, t=t: e.tensor_scalar(
                        tiles[3][:, 0:w], tiles[ib][:, 0:w], float(t), lut_b[:, t:t + 1],
                        ALU.is_equal, ALU.mult),
                        reads=(bufs[ib], b_lut), writes=(bufs[3],), sem=dsem_v)
                    p.do(eng, lambda e: e.tensor_tensor(
                        tiles[2][:, 0:w], tiles[2][:, 0:w], tiles[3][:, 0:w], ALU.add),
                        reads=(bufs[3],), writes=(bufs[2],), sem=dsem_v)
            p.do("scalar", lambda e: e.dma_start(
                out=dview(out_t, ht * 128 * ncols + c0, [[ncols, 128], [1, w]]),
                in_=tiles[2][:, 0:w]),
                reads=(bufs[2],), writes=(b_out,), inc_n=16, sem=dsem_st)
        return emit

    if PASS1:
        k = 0
        for (idx_t, out_t, b_out, nht, ncols) in (
                (gidx, gwt_loc, b_gwtl, DECROWS_GU // 128, INTER0),
                (uidx, uwt_loc, b_uwtl, DECROWS_GU // 128, INTER0),
                (didx, dwt_loc, b_dwtl, DECROWS_D // 128, D)):
            for ht in range(nht):
                c0 = 0
                while c0 < ncols:
                    w = min(DW, ncols - c0)
                    decode_units.append(mk_unit(idx_t, out_t, b_out, ht, c0, w, "vector", k))
                    c0 += w
                    k += 1

    def pump(n):
        while n > 0 and decode_units:
            decode_units.pop(0)()
            n -= 1

    agw_pending = [False]
    if PASS1:
        p.do("gpsimd", lambda e: e.dma_start(
            out=dview(wv_sb, 0, [[DT * DKV, 128], [DKV, DT], [1, DKV]]),
            in_=dview(wv, 0, [[DKV, 128], [128 * DKV, DT], [1, DKV]])),
            writes=(b_wv,), inc_n=16)

    def transpose_in(ntt, tt_base, dst, b_dst):
        for tt in range(ntt):
            for dt in range(DT):
                b = (tt * DT + dt) % 2
                p.do("tensor", lambda e, tt=tt, dt=dt, b=b: e.transpose(
                    mm_ps[b][:, 0:128],
                    xin_sb[:, tt * D + dt * 128: tt * D + (dt + 1) * 128],
                    ctx.id_sb[:, :]),
                    reads=(b_xin, ctx.b_id), writes=(b_mm[b],))
                to = tt_base + tt
                p.do("vector", lambda e, dt=dt, to=to, b=b: e.tensor_copy(
                    dst[:, dt * CH + to * 128: dt * CH + (to + 1) * 128], mm_ps[b][:, 0:128]),
                    reads=(b_mm[b],), writes=(b_dst,))

    def load_and_transpose(s, part, dst, b_dst):
        for q in range(4):
            p.do("sync", lambda e, s=s, part=part, q=q: e.dma_start(
                out=xin_sb[:, :],
                in_=dview(x_in, ((s * 2 + part) * CH + q * 128) * D,
                          [[D, 128], [1, D]])),
                writes=(b_xin,), inc_n=16)
            transpose_in(1, q, dst, b_dst)

    def kv_proj(dst_kt, b_dkt, dst_v, b_dv):
        for g in range(HK):
            buf = g % 2
            p.do("gpsimd", lambda e, g=g, buf=buf: e.dma_start(
                out=dview(wcol[buf], 0, [[DT * 128, 128], [128, DT], [1, 128]]),
                in_=dview(wk, g * 128, [[DKV, 128], [128 * DKV, DT], [1, 128]])),
                writes=(b_wcol[buf],), inc_n=16)
            bk = g % 2
            mm_group(ctx, mm_ps[bk][:, :],
                     [(wcol[buf][:, dt * 128:(dt + 1) * 128], xn[:, dt * CH:(dt + 1) * CH])
                      for dt in range(DT)],
                     (b_wcol[buf], b_xn), b_mm[bk])
            p.do("vector", lambda e, g=g, bk=bk, dst_kt=dst_kt: e.tensor_copy(
                dst_kt[:, g * CH:(g + 1) * CH], mm_ps[bk][:, :]),
                reads=(b_mm[bk],), writes=(b_dkt,))
        for tt in range(4):
            bk = tt % 2
            mm_group(ctx, av_ps[bk][:, 0:DKV],
                     [(xn[:, dt * CH + tt * 128: dt * CH + tt * 128 + 128],
                       wv_sb[:, dt * DKV:(dt + 1) * DKV]) for dt in range(DT)],
                     (b_wv, b_xn), b_av[bk])
            p.do("vector", lambda e, tt=tt, bk=bk, dst_v=dst_v: e.tensor_copy(
                dst_v[:, tt * DKV:(tt + 1) * DKV], av_ps[bk][:, 0:DKV]),
                reads=(b_av[bk],), writes=(b_dv,))

    def block_maxsq(src, b_src, nblk, out_col):
        for b in range(nblk):
            mm_group(ctx, l_ps[0:1, 0:CH],
                     [(ctx.ones_col_bf[:, :], src[:, b * CH:(b + 1) * CH])],
                     (b_src, ctx.b_ones), b_lp)
            p.do("vector", lambda e, b=b: e.reduce_max(scrow[0:1, 32 + b:33 + b], l_ps[0:1, 0:CH], AXL.X),
                 reads=(b_lp,), writes=(b_scrow,))
        p.do("vector", lambda e, nblk=nblk, out_col=out_col: e.reduce_max(
            scrow[0:1, out_col:out_col + 1], scrow[0:1, 32:32 + nblk], AXL.X),
            reads=(b_scrow,), writes=(b_scrow,))

    def emit_negm_from(cols):
        p.do("vector", lambda e, cols=cols: e.tensor_tensor(
            scrow[0:1, 2:3], scrow[0:1, cols[0]:cols[0] + 1], scrow[0:1, cols[1]:cols[1] + 1], ALU.mult),
            reads=(b_scrow,), writes=(b_scrow,))
        p.do("scalar", lambda e: e.activation(scrow[0:1, 3:4], scrow[0:1, 2:3], AF.Sqrt, bias=0.0, scale=1.0),
             reads=(b_scrow,), writes=(b_scrow,))
        p.do("vector", lambda e: e.tensor_scalar_mul(scrow[0:1, 4:5], scrow[0:1, 3:4], -1.0),
             reads=(b_scrow,), writes=(b_scrow,))
        mm_group(ctx, l_ps[:, 0:1], [(ctx.ones_row_f32[0:1, :], scrow[0:1, 4:5])],
                 (b_scrow, ctx.b_ones), b_lp)
        p.do("vector", lambda e: e.tensor_copy(negm[:, :], l_ps[:, 0:1]), reads=(b_lp,), writes=(b_negm,))

    for s in range(NCH):
      if not PASS1:
        continue
      if True:
        for part in (0, 1):
            load_and_transpose(s, part, xT, b_xT)
            emit_ln(ctx, xT, b_xT, xn, b_xn)
            pump(8)
            if part == 0:
                kv_proj(kt_prev, b_ktp, v_prev, b_vp)
            else:
                kv_proj(kt_cur, b_ktc, v_cur, b_vc)
            pump(8)

        for g in range(H):
            buf = g % 2
            p.do("gpsimd", lambda e, g=g, buf=buf: e.dma_start(
                out=dview(wcol[buf], 0, [[DT * 128, 128], [128, DT], [1, 128]]),
                in_=dview(wq, g * 128, [[D, 128], [128 * D, DT], [1, 128]])),
                writes=(b_wcol[buf],), inc_n=16)
            bk = g % 2
            mm_group(ctx, mm_ps[bk][:, :],
                     [(wcol[buf][:, dt * 128:(dt + 1) * 128], xn[:, dt * CH:(dt + 1) * CH])
                      for dt in range(DT)],
                     (b_wcol[buf], b_xn), b_mm[bk])
            p.do("scalar", lambda e, g=g, bk=bk: e.activation(
                qt0[:, g * CH:(g + 1) * CH], mm_ps[bk][:, :], AF.Copy, bias=0.0, scale=float(SCALE0)),
                reads=(b_mm[bk],), writes=(b_qt0,))

        p.do("sync", lambda e, s=s: e.dma_start(
            out=mask8[:, :],
            in_=dview(masks0, s * 8 * 128 * 512, [[512, 128], [128 * 512, 8], [1, 512]])),
            writes=(b_m8,), inc_n=16)

        sqs, b_sqs = ctx.ln_xb, ctx.b_ln_xb
        p.do("vector", lambda e: e.tensor_tensor(sqs[:, 0:HK * CH], kt_prev[:, :], kt_prev[:, :], ALU.mult),
             reads=(b_ktp,), writes=(b_sqs,))
        p.do("vector", lambda e: e.tensor_tensor(sqs[:, HK * CH:2 * HK * CH], kt_cur[:, :], kt_cur[:, :], ALU.mult),
             reads=(b_ktc,), writes=(b_sqs,))
        block_maxsq(sqs, b_sqs, 2 * HK, 0)
        p.do("vector", lambda e: e.tensor_tensor(sqs[:, 0:H * CH], qt0[:, :], qt0[:, :], ALU.mult),
             reads=(b_qt0,), writes=(b_sqs,))
        block_maxsq(sqs, b_sqs, H, 1)
        emit_negm_from((0, 1))

        for g in range(H):
            kg = g // 4
            for j in range(8):
                bk = j % 2
                src, bsrc = (kt_prev, b_ktp) if j < 4 else (kt_cur, b_ktc)
                jj = j % 4
                mm_group(ctx, mm_ps[bk][:, :],
                         [(src[:, kg * CH + jj * 128: kg * CH + (jj + 1) * 128],
                           qt0[:, g * CH:(g + 1) * CH])],
                         (bsrc, b_qt0), b_mm[bk])
                p.do("scalar", lambda e, j=j, bk=bk: e.activation(
                    ptb[:, j * CH:(j + 1) * CH], mm_ps[bk][:, :], AF.Exp,
                    bias=negm[:, 0:1], scale=1.0),
                    reads=(b_mm[bk], b_negm), writes=(b_ptb,))
                p.do("vector", lambda e, j=j: e.tensor_tensor(
                    ptb[:, j * CH:(j + 1) * CH], ptb[:, j * CH:(j + 1) * CH],
                    mask8[:, j * CH:(j + 1) * CH], ALU.mult),
                    reads=(b_m8,), writes=(b_ptb,))
            mm_group(ctx, l_ps[0:1, 0:CH],
                     [(ctx.ones_col_bf[:, :], ptb[:, j * CH:(j + 1) * CH]) for j in range(8)],
                     (b_ptb, ctx.b_ones), b_lp)
            p.do("vector", lambda e: e.reciprocal(linv[0:1, :], l_ps[0:1, 0:CH]),
                 reads=(b_lp,), writes=(b_linv,))
            abk = g % 2
            terms = []
            for j in range(8):
                vsrc = v_prev if j < 4 else v_cur
                jj = j % 4
                terms.append((vsrc[:, jj * DKV + kg * 128: jj * DKV + (kg + 1) * 128],
                              ptb[:, j * CH:(j + 1) * CH]))
            mm_group(ctx, av_ps[abk][:, :], terms, (b_vp, b_vc, b_ptb), b_av[abk])
            mm_group(ctx, l_ps[:, 0:CH], [(ctx.ones_row_f32[0:1, :], linv[0:1, :])],
                     (b_linv, ctx.b_ones), b_lp)
            p.do("vector", lambda e: e.tensor_copy(linvB[:, :], l_ps[:, 0:CH]),
                 reads=(b_lp,), writes=(b_linvB,))
            p.do("vector", lambda e, g=g, abk=abk: e.tensor_tensor(
                attn_out[:, g * CH:(g + 1) * CH], av_ps[abk][:, :], linvB[:, :], ALU.mult),
                reads=(b_av[abk], b_linvB), writes=(b_ao,))

        for o in range(DT):
            buf = o % 2
            p.do("gpsimd", lambda e, o=o, buf=buf: e.dma_start(
                out=dview(wcol[buf], 0, [[DT * 128, 128], [128, DT], [1, 128]]),
                in_=dview(wo, o * 128, [[D, 128], [128 * D, DT], [1, 128]])),
                writes=(b_wcol[buf],), inc_n=16)
            bk = o % 2
            mm_group(ctx, mm_ps[bk][:, :],
                     [(wcol[buf][:, g * 128:(g + 1) * 128], attn_out[:, g * CH:(g + 1) * CH])
                      for g in range(DT)],
                     (b_wcol[buf], b_ao), b_mm[bk])
            p.do("vector", lambda e, o=o, bk=bk: e.tensor_tensor(
                xT[:, o * CH:(o + 1) * CH], xT[:, o * CH:(o + 1) * CH], mm_ps[bk][:, :], ALU.add),
                reads=(b_mm[bk],), writes=(b_xT,))
        p.do("scalar", lambda e, s=s: e.dma_start(
            out=dview(x1_hbm, s * 128 * COLS, [[COLS, 128], [1, COLS]]),
            in_=xT[:, :]), reads=(b_xT,), writes=(b_x1h,), inc_n=16)
        pump(16)
        p.barrier(skip=("vector", "gpsimd", "scalar"))

    if upto == 1 or L1:
        pump(10 ** 9)
        finish(p, es, nc)
        return nc
    pump(10 ** 9)

    for s in range(NCH):
      if not PASS2:
        continue
      if True:
        p.do("sync", lambda e, s=s: e.dma_start(
            out=xT[:, :], in_=dview(x1_hbm, s * 128 * COLS, [[COLS, 128], [1, COLS]])),
            reads=(b_x1h,), writes=(b_xT,), inc_n=16)
        emit_ln(ctx, xT, b_xT, xn, b_xn)
        for it in range(IT0):
            gb, ub = it % 2, 2 + it % 2
            p.do("sync", lambda e, it=it, gb=gb: e.dma_start(
                out=dview(wcol[gb], 0, [[DT * 128, 128], [128, DT], [1, 128]]),
                in_=dview(gwt_g, it * 128, [[INTER0, 128], [128 * INTER0, DT], [1, 128]])),
                reads=(b_gwtg if L2 else b_gwtl,), writes=(b_wcol[gb],), inc_n=16)
            mm_group(ctx, mm_ps[0][:, :],
                     [(wcol[gb][:, dt * 128:(dt + 1) * 128], xn[:, dt * CH:(dt + 1) * CH])
                      for dt in range(DT)],
                     (b_wcol[gb], b_xn), b_mm[0])
            p.do("scalar", lambda e, it=it: e.activation(
                h0[:, it * CH:(it + 1) * CH], mm_ps[0][:, :], AF.Silu, bias=0.0, scale=1.0),
                reads=(b_mm[0],), writes=(b_h0,))
            p.do("sync", lambda e, it=it, ub=ub: e.dma_start(
                out=dview(wcol[ub], 0, [[DT * 128, 128], [128, DT], [1, 128]]),
                in_=dview(uwt_g, it * 128, [[INTER0, 128], [128 * INTER0, DT], [1, 128]])),
                reads=(b_gwtg if L2 else b_uwtl,), writes=(b_wcol[ub],), inc_n=16)
            mm_group(ctx, mm_ps[1][:, :],
                     [(wcol[ub][:, dt * 128:(dt + 1) * 128], xn[:, dt * CH:(dt + 1) * CH])
                      for dt in range(DT)],
                     (b_wcol[ub], b_xn), b_mm[1])
            p.do("vector", lambda e, it=it: e.tensor_tensor(
                h0[:, it * CH:(it + 1) * CH], h0[:, it * CH:(it + 1) * CH], mm_ps[1][:, :], ALU.mult),
                reads=(b_mm[1],), writes=(b_h0,))
        pump(10)
        for o in range(DT):
            db = o % 2
            p.do("sync", lambda e, o=o, db=db: e.dma_start(
                out=dview(wdcolA[db], 0, [[IT0 * 128, 128], [128, IT0], [1, 128]]),
                in_=dview(dwt_g, o * 128, [[D, 128], [128 * D, IT0], [1, 128]])),
                reads=(b_gwtg if L2 else b_dwtl,), writes=(b_wdA[db],), inc_n=16)
            bk = o % 2
            mm_group(ctx, mm_ps[bk][:, :],
                     [(wdcolA[db][:, ht * 128:(ht + 1) * 128], h0[:, ht * CH:(ht + 1) * CH])
                      for ht in range(IT0)],
                     (b_wdA[db], b_h0), b_mm[bk])
            p.do("vector", lambda e, o=o, bk=bk: e.tensor_tensor(
                xT[:, o * CH:(o + 1) * CH], xT[:, o * CH:(o + 1) * CH], mm_ps[bk][:, :], ALU.add),
                reads=(b_mm[bk],), writes=(b_xT,))
        p.do("scalar", lambda e, s=s: e.dma_start(
            out=dview(x2_hbm, s * 128 * COLS, [[COLS, 128], [1, COLS]]),
            in_=xT[:, :]), reads=(b_xT,), writes=(b_x2h,), inc_n=16)
        if upto == 2:
            p.barrier()
            continue
        p.barrier()

        emit_ln(ctx, xT, b_xT, None, None, out_u=u_sb, b_u=b_u)
        p.do("scalar", lambda e, s=s: e.dma_start(
            out=dview(u_hbm, s * 128 * COLS, [[COLS, 128], [1, COLS]]),
            in_=u_sb[:, :]), reads=(b_u,), writes=(b_uh,), inc_n=16)
        emit_ln(ctx, u_sb, b_u, xn, b_xn, gamma=gam_sb, beta=bet_sb, b_gb=b_gb)
        for proj in (0, 1):
            for o in range(DT):
                buf = o % 2
                p.do("gpsimd", lambda e, proj=proj, o=o, buf=buf: e.dma_start(
                    out=dview(wcol[buf], 0, [[DT * 128, 128], [128, DT], [1, 128]]),
                    in_=dview(qkv_t, proj * D + o * 128, [[3 * D, 128], [128 * 3 * D, DT], [1, 128]])),
                    writes=(b_wcol[buf],), inc_n=16)
                bk = o % 2
                mm_group(ctx, mm_ps[bk][:, :],
                         [(wcol[buf][:, dt * 128:(dt + 1) * 128], xn[:, dt * CH:(dt + 1) * CH])
                          for dt in range(DT)],
                         (b_wcol[buf], b_xn), b_mm[bk])
                sb_ = stgb[o % 2]
                bs = b_stgb[o % 2]
                p.do("scalar", lambda e, bk=bk, sb_=sb_, proj=proj: e.activation(
                    sb_[:, :], mm_ps[bk][:, :], AF.Copy, bias=0.0,
                    scale=float(SCALE1) if proj == 0 else 1.0),
                    reads=(b_mm[bk],), writes=(bs,))
                if proj == 0:
                    p.do("scalar", lambda e, s=s, o=o, sb_=sb_: e.dma_start(
                        out=dview(qt_hbm, s * 128 * COLS + o * CH, [[COLS, 128], [1, CH]]),
                        in_=sb_[:, :]), reads=(bs,), writes=(b_qth,), inc_n=16)
                else:
                    p.do("scalar", lambda e, s=s, o=o, sb_=sb_: e.dma_start(
                        out=dview(kt_loc, (s * 4 * 128) * D + o * 128,
                                  [[D, 128], [128 * D, 4], [1, 128]]),
                        in_=sb_[:, :]), reads=(bs,), writes=(b_ktl,), inc_n=16)
                    p.do("vector", lambda e, sb_=sb_: e.tensor_tensor(
                        ctx.ln_xb[:, 0:CH], sb_[:, :], sb_[:, :], ALU.mult),
                        reads=(bs,), writes=(ctx.b_ln_xb,))
                    mm_group(ctx, l_ps[0:1, 0:CH],
                             [(ctx.ones_col_bf[:, :], ctx.ln_xb[:, 0:CH])],
                             (ctx.b_ln_xb, ctx.b_ones), b_lp,
                             start=(o == 0), stop=(o == DT - 1))
        p.do("vector", lambda e: e.reduce_max(scrow[0:1, 8:9], l_ps[0:1, 0:CH], AXL.X),
             reads=(b_lp,), writes=(b_scrow,))
        if s == 0:
            p.do("vector", lambda e: e.tensor_copy(scrow[0:1, 9:10], scrow[0:1, 8:9]),
                 reads=(b_scrow,), writes=(b_scrow,))
        else:
            p.do("vector", lambda e: e.tensor_tensor(scrow[0:1, 9:10], scrow[0:1, 9:10], scrow[0:1, 8:9], ALU.max),
                 reads=(b_scrow,), writes=(b_scrow,))
        for dvc in range(4):
            p.do("gpsimd", lambda e, dvc=dvc: e.dma_start(
                out=dview(vslab, 0, [[DT * CH, 128], [CH, DT], [1, CH]]),
                in_=dview(qkv_t, 2 * D + dvc * CH, [[3 * D, 128], [128 * 3 * D, DT], [1, CH]])),
                writes=(b_vslab,), inc_n=16)
            for tt in range(4):
                bk = tt % 2
                mm_group(ctx, av_ps[bk][:, :],
                         [(xn[:, dt * CH + tt * 128: dt * CH + tt * 128 + 128],
                           vslab[:, dt * CH:(dt + 1) * CH]) for dt in range(DT)],
                         (b_vslab, b_xn), b_av[bk])
                sb_ = stgb[tt % 2]
                bs = b_stgb[tt % 2]
                p.do("scalar", lambda e, bk=bk, sb_=sb_: e.activation(
                    sb_[:, :], av_ps[bk][:, :], AF.Copy, bias=0.0, scale=1.0),
                    reads=(b_av[bk],), writes=(bs,))
                p.do("scalar", lambda e, s=s, tt=tt, dvc=dvc, sb_=sb_: e.dma_start(
                    out=dview(v_loc, (s * CH + tt * 128) * D + dvc * CH, [[D, 128], [1, CH]]),
                    in_=sb_[:, :]), reads=(bs,), writes=(b_vl,), inc_n=16)
        p.barrier()

    if upto == 1 or upto == 2:
        finish(p, es, nc)
        return nc

    if not L3:
      p.do("gpsimd", lambda e: e.memset(stats_sb[:, :], 0.0), writes=(b_stats,))
      p.do("vector", lambda e: e.tensor_copy(stats_sb[0:1, 0:1], scrow[0:1, 9:10]),
           reads=(b_scrow,), writes=(b_stats,))
      p.do("scalar", lambda e: e.dma_start(out=stats_loc.ap(), in_=stats_sb[:, :]),
           reads=(b_stats,), writes=(b_statl,), inc_n=16)
    if LA or L2:
        finish(p, es, nc)
        return nc
    if FUSED:
        for (il, ol, bl) in ((kt_loc, kt_g, b_ktl), (v_loc, v_g, b_vl),
                             (stats_loc, stats_g, b_statl)):
            p.do("gpsimd", lambda e, il=il, ol=ol: e.collective_compute(
                "AllGather", ALU.bypass, replica_groups=[list(range(NCORE))],
                ins=[il.ap().opt()], outs=[ol.ap().opt()]),
                reads=(bl,), writes=(b_kvg,), inc_n=16, sem=s_cc)
        p.barrier(extra=(s_cc,))

    p.do("sync", lambda e: e.dma_start(out=cpos_sb[:, :], in_=cpos_t.ap()),
         writes=(b_cpos,), inc_n=16)
    p.do("sync", lambda e: e.dma_start(out=frow_sb[:, :], in_=frow_t.ap()),
         writes=(b_frow,), inc_n=16)
    p.do("sync", lambda e: e.dma_start(
        out=dview(ctx.ln_st, 0, [[4 * CH, 1], [1, NCORE * 128]]),
        in_=dview(stats_g, 0, [[NCORE * 128, 1], [1, NCORE * 128]])),
        reads=(b_kvg,), writes=(ctx.b_ln_st,), inc_n=16)
    p.do("vector", lambda e: e.reduce_max(scrow[0:1, 10:11], ctx.ln_st[0:1, 0:NCORE * 128], AXL.X),
         reads=(ctx.b_ln_st,), writes=(b_scrow,))

    jflat = [0]
    for s in range(NCH):
        p.do("sync", lambda e, s=s: e.dma_start(
            out=qtb[:, :], in_=dview(qt_hbm, s * 128 * COLS, [[COLS, 128], [1, COLS]])),
            reads=(b_qth,), writes=(b_qtb,), inc_n=16)
        p.do("vector", lambda e: e.tensor_tensor(pts[:, 0:COLS], qtb[:, :], qtb[:, :], ALU.mult),
             reads=(b_qtb,), writes=(b_pts,))
        mm_group(ctx, l_ps[0:1, 0:CH],
                 [(ctx.ones_col_bf[:, :], pts[:, dt * CH:(dt + 1) * CH]) for dt in range(DT)],
                 (b_pts, ctx.b_ones), b_lp)
        p.do("vector", lambda e: e.reduce_max(scrow[0:1, 11:12], l_ps[0:1, 0:CH], AXL.X),
             reads=(b_lp,), writes=(b_scrow,))
        emit_negm_from((10, 11))
        p.do("gpsimd", lambda e: e.memset(acc[:, :], 0.0), writes=(b_acc,))

        nseg_s = SLOT_SEGS[s]
        for r in range(nseg_s):
            first_seg = (r == 0)
            for u in range(SEGT):
                jf = jflat[0] + (r * SEGT + u)
                bu = u % 2
                t_glob = r * SEGT + u
                jk = t_glob // 4
                rk, sk = rank_slot_of_chunk(jk)
                kt_off = (rk * 2048 + (sk * 4 + t_glob % 4) * 128) * D
                v_off = (rk * 2048 + sk * CH + (t_glob % 4) * 128) * D
                p.do("scalar", lambda e, kt_off=kt_off, bu=bu: e.dma_start(
                    out=dview(ktile[bu], 0, [[DT * 128, 128], [1, DT * 128]]),
                    in_=dview(kt_g, kt_off, [[D, 128], [1, D]])),
                    reads=(b_kvg,), writes=(b_ktile[bu],), inc_n=16)
                p.do("sync", lambda e, v_off=v_off, u=u: e.dma_start(
                    out=dview(vbig, u * DT * 128, [[SEGT * DT * 128, 128], [1, DT * 128]]),
                    in_=dview(v_g, v_off, [[D, 128], [1, D]])),
                    reads=(b_kvg,), writes=(b_vbig,), inc_n=16)

                mm_group(ctx, mm_ps[bu][:, :],
                         [(ktile[bu][:, dt * 128:(dt + 1) * 128], qtb[:, dt * CH:(dt + 1) * CH])
                          for dt in range(DT)],
                         (b_ktile[bu], b_qtb), b_mm[bu])
                p.do("scalar", lambda e, u=u, bu=bu: e.activation(
                    pts[:, u * CH:(u + 1) * CH], mm_ps[bu][:, :], AF.Exp,
                    bias=negm[:, 0:1], scale=1.0),
                    reads=(b_mm[bu], b_negm), writes=(b_pts,))
                p.do("vector", lambda e, u=u, jf=jf: e.scalar_tensor_tensor(
                    pts[:, u * CH:(u + 1) * CH], frow_sb[:, :], cpos_sb[:, jf:jf + 1],
                    pts[:, u * CH:(u + 1) * CH], ALU.is_ge, ALU.mult),
                    reads=(b_frow, b_cpos), writes=(b_pts,))
                mm_group(ctx, l_ps[0:1, 0:CH],
                         [(ctx.ones_col_bf[:, :], pts[:, u * CH:(u + 1) * CH])],
                         (b_pts, ctx.b_ones), b_lp,
                         start=(first_seg and u == 0), stop=(r == nseg_s - 1 and u == SEGT - 1))
            for o in range(DT):
                bk = o % 2
                mm_group(ctx, av_ps[bk][:, :],
                         [(vbig[:, u2 * DT * 128 + o * 128: u2 * DT * 128 + o * 128 + 128],
                           pts[:, u2 * CH:(u2 + 1) * CH]) for u2 in range(SEGT)],
                         (b_vbig, b_pts), b_av[bk])
                p.do("vector", lambda e, o=o, bk=bk: e.tensor_tensor(
                    acc[:, o * CH:(o + 1) * CH], acc[:, o * CH:(o + 1) * CH],
                    av_ps[bk][:, :], ALU.add),
                    reads=(b_av[bk],), writes=(b_acc,))
        jflat[0] += nseg_s * SEGT

        p.do("vector", lambda e: e.reciprocal(linv[0:1, :], l_ps[0:1, 0:CH]),
             reads=(b_lp,), writes=(b_linv,))
        mm_group(ctx, l_ps[:, 0:CH], [(ctx.ones_row_f32[0:1, :], linv[0:1, :])],
                 (b_linv, ctx.b_ones), b_lp)
        p.do("vector", lambda e: e.tensor_copy(linvB[:, :], l_ps[:, 0:CH]),
             reads=(b_lp,), writes=(b_linvB,))
        for o in range(DT):
            p.do("vector", lambda e, o=o: e.tensor_tensor(
                pts[:, o * CH:(o + 1) * CH], acc[:, o * CH:(o + 1) * CH], linvB[:, :], ALU.mult),
                reads=(b_acc, b_linvB), writes=(b_pts,))
        for o in range(DT):
            buf = o % 2
            p.do("gpsimd", lambda e, o=o, buf=buf: e.dma_start(
                out=dview(wcol[buf], 0, [[DT * 128, 128], [128, DT], [1, 128]]),
                in_=dview(oproj_t, o * 128, [[D, 128], [128 * D, DT], [1, 128]])),
                writes=(b_wcol[buf],), inc_n=16)
            bk = o % 2
            mm_group(ctx, mm_ps[bk][:, :],
                     [(wcol[buf][:, g * 128:(g + 1) * 128], pts[:, g * CH:(g + 1) * CH])
                      for g in range(DT)],
                     (b_wcol[buf], b_pts), b_mm[bk])
            xb_, ub_ = xu_t[2 * (o % 2)], xu_t[2 * (o % 2) + 1]
            bxb, bub = b_xu[2 * (o % 2)], b_xu[2 * (o % 2) + 1]
            p.do("scalar", lambda e, s=s, o=o, xb_=xb_: e.dma_start(
                out=xb_[:, :], in_=dview(x2_hbm, s * 128 * COLS + o * CH, [[COLS, 128], [1, CH]])),
                reads=(b_x2h,), writes=(bxb,), inc_n=16)
            p.do("scalar", lambda e, s=s, o=o, ub_=ub_: e.dma_start(
                out=ub_[:, :], in_=dview(u_hbm, s * 128 * COLS + o * CH, [[COLS, 128], [1, CH]])),
                reads=(b_uh,), writes=(bub,), inc_n=16)
            st_ = x3stg[o % 2]
            bst = b_x3s[o % 2]
            p.do("vector", lambda e, bk=bk, xb_=xb_, st_=st_: e.tensor_tensor(
                st_[:, :], mm_ps[bk][:, :], xb_[:, :], ALU.add),
                reads=(b_mm[bk], bxb), writes=(bst,))
            p.do("vector", lambda e, ub_=ub_, st_=st_: e.tensor_tensor(
                st_[:, :], st_[:, :], ub_[:, :], ALU.add),
                reads=(bub,), writes=(bst,))
            p.do("scalar", lambda e, s=s, o=o, st_=st_: e.dma_start(
                out=dview(x3_hbm, s * 128 * COLS + o * CH, [[COLS, 128], [1, CH]]),
                in_=st_[:, :]), reads=(bst,), writes=(b_x3h,), inc_n=16)
        p.barrier()

    if upto == 3:
        finish(p, es, nc)
        return nc

    for s in range(NCH):
        p.do("sync", lambda e, s=s: e.dma_start(
            out=xT[:, :], in_=dview(x3_hbm, s * 128 * COLS, [[COLS, 128], [1, COLS]])),
            reads=(b_x3h,), writes=(b_xT,), inc_n=16)
        emit_ln(ctx, xT, b_xT, xn, b_xn)
        for it in range(IT1):
            gb, ub = it % 2, 2 + it % 2
            p.do("gpsimd", lambda e, it=it, gb=gb: e.dma_start(
                out=dview(wcol[gb], 0, [[DT * 128, 128], [128, DT], [1, 128]]),
                in_=dview(wg1, it * 128, [[INTER1, 128], [128 * INTER1, DT], [1, 128]])),
                writes=(b_wcol[gb],), inc_n=16)
            mm_group(ctx, mm_ps[0][:, :],
                     [(wcol[gb][:, dt * 128:(dt + 1) * 128], xn[:, dt * CH:(dt + 1) * CH])
                      for dt in range(DT)],
                     (b_wcol[gb], b_xn), b_mm[0])
            p.do("scalar", lambda e, it=it: e.activation(
                h1[:, it * CH:(it + 1) * CH], mm_ps[0][:, :], AF.Silu, bias=0.0,
                scale=sg1_sb[:, it:it + 1]),
                reads=(b_mm[0], b_sc1), writes=(b_h1,))
            p.do("gpsimd", lambda e, it=it, ub=ub: e.dma_start(
                out=dview(wcol[ub], 0, [[DT * 128, 128], [128, DT], [1, 128]]),
                in_=dview(wu1, it * 128, [[INTER1, 128], [128 * INTER1, DT], [1, 128]])),
                writes=(b_wcol[ub],), inc_n=16)
            mm_group(ctx, mm_ps[1][:, :],
                     [(wcol[ub][:, dt * 128:(dt + 1) * 128], xn[:, dt * CH:(dt + 1) * CH])
                      for dt in range(DT)],
                     (b_wcol[ub], b_xn), b_mm[1])
            p.do("vector", lambda e, it=it: e.scalar_tensor_tensor(
                h1[:, it * CH:(it + 1) * CH], mm_ps[1][:, :], su1_sb[:, it:it + 1],
                h1[:, it * CH:(it + 1) * CH], ALU.mult, ALU.mult),
                reads=(b_mm[1], b_sc1), writes=(b_h1,))
        for o in range(DT):
            db = o % 2
            p.do("gpsimd", lambda e, o=o, db=db: e.dma_start(
                out=dview(wdcolC[db], 0, [[IT1 * 128, 128], [128, IT1], [1, 128]]),
                in_=dview(wd1, o * 128, [[D, 128], [128 * D, IT1], [1, 128]])),
                writes=(b_wdC[db],), inc_n=16)
            bk = o % 2
            mm_group(ctx, mm_ps[bk][:, :],
                     [(wdcolC[db][:, ht * 128:(ht + 1) * 128], h1[:, ht * CH:(ht + 1) * CH])
                      for ht in range(IT1)],
                     (b_wdC[db], b_h1), b_mm[bk])
            p.do("vector", lambda e, o=o, bk=bk: e.scalar_tensor_tensor(
                xT[:, o * CH:(o + 1) * CH], mm_ps[bk][:, :], sd1_sb[:, o:o + 1],
                xT[:, o * CH:(o + 1) * CH], ALU.mult, ALU.add),
                reads=(b_mm[bk], b_sc1), writes=(b_xT,))
        for tt in range(4):
            for dt in range(DT):
                bk = dt % 2
                p.do("tensor", lambda e, tt=tt, dt=dt, bk=bk: e.transpose(
                    mm_ps[bk][:, 0:128],
                    xT[:, dt * CH + tt * 128: dt * CH + (tt + 1) * 128],
                    ctx.id_sb[:, :]),
                    reads=(b_xT, ctx.b_id), writes=(b_mm[bk],))
                p.do("vector", lambda e, dt=dt, bk=bk: e.tensor_copy(
                    xout[:, dt * 128:(dt + 1) * 128], mm_ps[bk][:, 0:128]),
                    reads=(b_mm[bk],), writes=(b_xout,))
            p.do("scalar", lambda e, s=s, tt=tt: e.dma_start(
                out=dview(out_rows, (s * CH + tt * 128) * D, [[D, 128], [1, D]]),
                in_=xout[:, :]), reads=(b_xout,), inc_n=16)
        p.barrier()

    finish(p, es, nc)
    return nc


def finish(p, es, nc):
    p.barrier()
    emit_block(nc, p)
    es.close()


IN_SHAPES = dict(
    x_in=((NCH, 2, CH, D), np.float32), wq=((D, D), np.float32),
    wk=((D, DKV), np.float32), wv=((D, DKV), np.float32), wo=((D, D), np.float32),
    masks0=((NCH, 8, 128, 512), BF), ident=((128, 128), np.float32),
    lut=((1, LUTN), np.float32), gidx=((D, INTER0), np.int32),
    uidx=((D, INTER0), np.int32), didx=((INTER0, D), np.int32),
    qkv=((D, 3 * D), np.float32), oproj=((D, D), np.float32),
    gam=((128, DT), np.float32), bet=((128, DT), np.float32),
    wg1=((D, INTER1), np.int8), wu1=((D, INTER1), np.int8),
    wd1=((INTER1, D), np.int8),
    sg1=((128, IT1), np.float32), su1=((128, IT1), np.float32),
    sd1=((128, DT), np.float32),
    cpos=((128, NTILE_B), np.float16), frow=((128, 512), np.float16),
    x2_hbm=((NCH, 128, COLS), np.float32), u_hbm=((NCH, 128, COLS), np.float32),
    qt_hbm=((NCH, 128, COLS), BF),
    kt_g=((NCORE * D, NCH * CH), BF), v_g=((NCORE * NCH * CH, D), BF),
    stats_g=((NCORE, 128), np.float32),
)


_NPDT = {"float32": np.float32, "bfloat16": BF, "int32": np.int32,
         "int8": np.int8, "float16": np.float16}


def _maps_for(nc, per_core):
    import concourse.mybir as mb
    names = []
    for alloc in nc.m.functions[0].allocations:
        if isinstance(alloc, mb.MemoryLocationSet) and alloc.kind == "ExternalInput":
            n = alloc.memorylocations[0].name
            if n == "partition_id":
                continue
            names.append((n, list(alloc.tensor_shape), str(alloc.dtype).split(".")[-1]))
    maps = []
    for core in range(NCORE):
        m = {}
        for n, shape, dtname in names:
            if n in per_core[core]:
                m[n] = per_core[core][n]
            else:
                m[n] = np.zeros(shape, _NPDT[dtname])
        maps.append(m)
    return maps


def kernel(**inputs):
    import os
    x = np.asarray(inputs["L0_attn_x"]).reshape(T, D)
    wq = np.ascontiguousarray(inputs["L0_attn_Wq"], dtype=np.float32)
    wk = np.ascontiguousarray(inputs["L0_attn_Wk"], dtype=np.float32)
    wv = np.ascontiguousarray(inputs["L0_attn_Wv"], dtype=np.float32)
    wo = np.ascontiguousarray(inputs["L0_attn_Wo"], dtype=np.float32)
    lut = np.ascontiguousarray(inputs["L0_ffn_lut"], dtype=np.float32).reshape(1, LUTN)
    gidx_full = np.asarray(inputs["L0_ffn_gate_idx"], dtype=np.int32)
    uidx_full = np.asarray(inputs["L0_ffn_up_idx"], dtype=np.int32)
    didx_full = np.asarray(inputs["L0_ffn_down_idx"], dtype=np.int32)
    qkv = np.ascontiguousarray(inputs["L1_attn_qkv"], dtype=np.float32)
    oproj = np.ascontiguousarray(inputs["L1_attn_o_proj"], dtype=np.float32)
    gam = np.ascontiguousarray(np.asarray(inputs["L1_attn_gamma"], dtype=np.float32).reshape(DT, 128).T)
    bet = np.ascontiguousarray(np.asarray(inputs["L1_attn_beta"], dtype=np.float32).reshape(DT, 128).T)
    wg1 = np.ascontiguousarray(inputs["L1_ffn_w_gate"])
    wu1 = np.ascontiguousarray(inputs["L1_ffn_w_up"])
    wd1 = np.ascontiguousarray(inputs["L1_ffn_w_down"])
    sg1 = np.ascontiguousarray(np.asarray(inputs["L1_ffn_s_gate"], dtype=np.float32).reshape(IT1, 128).T)
    su1 = np.ascontiguousarray(np.asarray(inputs["L1_ffn_s_up"], dtype=np.float32).reshape(IT1, 128).T)
    sd1 = np.ascontiguousarray(np.asarray(inputs["L1_ffn_s_down"], dtype=np.float32).reshape(DT, 128).T)
    ident = np.eye(128, dtype=np.float32)

    xsl = []
    for core in range(NCORE):
        a = np.zeros((NCH, 2, CH, D), dtype=np.float32)
        for s in range(NCH):
            j = chunk_of_slot(core, s)
            a[s, 1] = x[j * CH:(j + 1) * CH]
            if j > 0:
                a[s, 0] = x[(j - 1) * CH:j * CH]
        xsl.append(a)

    mode = os.environ.get("MK_MODE", "split")
    cores = list(range(NCORE))
    if mode == "fused":
        upto = int(os.environ.get("MK_UPTO", "4"))
        nc = build(upto=upto, mode="fused")
        per_core = [dict(
            x_in=xsl[c], wq=wq, wk=wk, wv=wv, wo=wo,
            masks0=host_masks_l0(c), ident=ident, lut=lut,
            gidx=np.ascontiguousarray(gidx_full.T),
            uidx=np.ascontiguousarray(uidx_full.T),
            didx=np.ascontiguousarray(didx_full.T),
            qkv=qkv, oproj=oproj, gam=gam, bet=bet,
            wg1=wg1, wu1=wu1, wd1=wd1, sg1=sg1, su1=su1, sd1=sd1,
            cpos=host_cpos(c), frow=host_frow()) for c in cores]
        res = run_bass_kernel_spmd(nc, _maps_for(nc, per_core), core_ids=cores)
        if upto >= 4:
            out = np.zeros((T, D), dtype=np.float32)
            for c in cores:
                r = res.results[c]["out_rows"]
                for s in range(NCH):
                    j = chunk_of_slot(c, s)
                    out[j * CH:(j + 1) * CH] = r[s]
            return out.reshape(1, T, D)
        key = {1: "x1_hbm", 2: "x2_hbm", 3: "x3_hbm"}[upto]
        out = np.zeros((T, D), dtype=np.float32)
        for c in cores:
            r = res.results[c][key]
            for s in range(NCH):
                j = chunk_of_slot(c, s)
                t = r[s].reshape(128, DT, CH)
                out[j * CH:(j + 1) * CH] = t.transpose(2, 1, 0).reshape(CH, D)
        return out.reshape(1, T, D)

    gidxT = np.ascontiguousarray(gidx_full.T)
    uidxT = np.ascontiguousarray(uidx_full.T)
    didxT = np.ascontiguousarray(didx_full.T)
    didxT_pad = np.zeros((NCORE * 6 * 128, D), np.int32)
    didxT_pad[:INTER0] = didxT
    nc1 = build(mode="L1")
    pc1 = [dict(
        x_in=xsl[c], wq=wq, wk=wk, wv=wv, wo=wo,
        masks0=host_masks_l0(c), ident=ident, lut=lut,
        gidx=np.ascontiguousarray(gidxT[c * DEC:(c + 1) * DEC]),
        uidx=np.ascontiguousarray(uidxT[c * DEC:(c + 1) * DEC]),
        didx=np.ascontiguousarray(didxT_pad[c * 6 * 128:(c + 1) * 6 * 128]),
    ) for c in cores]
    r1 = run_bass_kernel_spmd(nc1, _maps_for(nc1, pc1), core_ids=cores).results
    sup = int(os.environ.get("MK_SPLIT_UPTO", "3"))

    def asm_t(key, rr):
        o = np.zeros((T, D), dtype=np.float32)
        for c in cores:
            r = rr[c][key]
            for s in range(NCH):
                j = chunk_of_slot(c, s)
                t = r[s].reshape(128, DT, CH)
                o[j * CH:(j + 1) * CH] = t.transpose(2, 1, 0).reshape(CH, D)
        return o.reshape(1, T, D)

    if sup == 1:
        return asm_t("x1_hbm", r1)
    gwt = np.concatenate([r1[c]["gwt_loc"] for c in cores], axis=0)
    uwt = np.concatenate([r1[c]["uwt_loc"] for c in cores], axis=0)
    dwt = np.concatenate([r1[c]["dwt_loc"] for c in cores], axis=0)[:INTER0]

    nc2 = build(mode="L2")
    pc2 = [dict(
        x1_hbm=r1[c]["x1_hbm"], gwt_g=gwt, uwt_g=uwt, dwt_g=dwt,
        lut=lut, qkv=qkv, gam=gam, bet=bet,
    ) for c in cores]
    r2 = run_bass_kernel_spmd(nc2, _maps_for(nc2, pc2), core_ids=cores).results
    if sup == 2:
        return asm_t("x2_hbm", r2)
    kt_g = np.concatenate([r2[c]["kt_loc"] for c in cores], axis=0)
    v_g = np.concatenate([r2[c]["v_loc"] for c in cores], axis=0)
    stats_g = np.concatenate([r2[c]["stats_loc"] for c in cores], axis=0)

    nc3 = build(mode="L3")
    pc3 = [dict(
        x2_hbm=r2[c]["x2_hbm"], u_hbm=r2[c]["u_hbm"], qt_hbm=r2[c]["qt_hbm"],
        kt_g=kt_g, v_g=v_g, stats_g=stats_g,
        oproj=oproj, ident=ident,
        wg1=wg1, wu1=wu1, wd1=wd1, sg1=sg1, su1=su1, sd1=sd1,
        cpos=host_cpos(c), frow=host_frow(),
    ) for c in cores]
    r3 = run_bass_kernel_spmd(nc3, _maps_for(nc3, pc3), core_ids=cores).results
    out = np.zeros((T, D), dtype=np.float32)
    for c in cores:
        r = r3[c]["out_rows"]
        for s in range(NCH):
            j = chunk_of_slot(c, s)
            out[j * CH:(j + 1) * CH] = r[s]
    return out.reshape(1, T, D)
